# version 09
# baseline (speedup 1.0000x reference)
import numpy as np
import concourse.bass as bass
import concourse.mybir as mybir
from concourse.bass_utils import run_bass_kernel_spmd
from contextlib import ExitStack

F32 = mybir.dt.float32
BF16 = mybir.dt.bfloat16
AF = mybir.ActivationFunctionType
ALU = mybir.AluOpType
AX = mybir.AxisListType

B_LOC = 2
D = 1024
N = 16
H = 4096
TC = 256
NDT = D // 128
NHT = H // 128
MAGIC = float(1.5 * 2 ** 23)

ENGS = ("v", "a", "t", "g", "s")


class Sched:

    def __init__(self, nc, stack):
        self.nc = nc
        self.q = {e: [] for e in ENGS}
        self.sem = {e: stack.enter_context(nc.semaphore(f"ms_{e}")) for e in ENGS}
        self.cnt = {e: 0 for e in ENGS}
        self.emitted = {e: 0 for e in ENGS}
        self.flushed = {e: 0 for e in ENGS}
        self.dsem = {e: stack.enter_context(nc.semaphore(f"ds_{e}")) for e in ("s", "s1", "a", "g")}
        self.dcnt = {e: 0 for e in ("s", "s1", "a", "g")}
        self.squeue_alias = "s"

    def emit(self, eng, fn):
        self.q[eng].append(["i", fn])
        self.emitted[eng] += 1

    def milestone(self, eng):
        if self.emitted[eng] == self.flushed[eng]:
            return self.cnt[eng]
        for i in range(len(self.q[eng]) - 1, -1, -1):
            if self.q[eng][i][0] == "i":
                fn = self.q[eng][i][1]
                sem = self.sem[eng]

                def wrapped(E, old=fn, sem=sem):
                    ins = old(E)
                    ins.then_inc(sem, 1)
                    return ins
                self.q[eng][i][1] = wrapped
                break
        self.cnt[eng] += 1
        self.flushed[eng] = self.emitted[eng]
        return self.cnt[eng]

    def vguard(self, eng="v"):
        c = self.milestone(eng)
        self.wait(eng, self.sem[eng], c)

    def wait(self, eng, sem, val):
        if val <= 0:
            return
        self.q[eng].append(["w", (sem, val)])

    def need(self, consumer, producer):
        if producer == consumer:
            return
        c = self.milestone(producer)
        self.wait(consumer, self.sem[producer], c)

    def fence_all(self, engines=("v", "a", "t")):
        cs = {e: self.milestone(e) for e in engines}
        for e in engines:
            for p in engines:
                if p != e:
                    self.wait(e, self.sem[p], cs[p])

    def dma(self, queue, out, in_, **kw):
        slot = self.squeue_alias if queue == "s" else queue
        sem = self.dsem[slot]

        def f(E):
            return E.dma_start(out=out, in_=in_, **kw).then_inc(sem, 16)
        self.q[queue].append(["i", f])
        self.emitted[queue] += 1
        self.dcnt[slot] += 16
        return (sem, self.dcnt[slot])

    def wait_dma(self, eng, queue_or_token, val=None):
        if isinstance(queue_or_token, tuple):
            sem, v = queue_or_token
        elif isinstance(val, tuple):
            sem, v = val
        else:
            slot = self.squeue_alias if queue_or_token == "s" else queue_or_token
            sem, v = self.dsem[slot], val
        self.wait(eng, sem, v)

    def run(self, block):
        def mk(eng):
            def body(E):
                for kind, payload in self.q[eng]:
                    if kind == "i":
                        payload(E)
                    else:
                        sem, val = payload
                        E.wait_ge(sem, val)
            return body
        block.vector(mk("v"))
        block.scalar(mk("a"))
        block.tensor(mk("t"))
        block.gpsimd(mk("g"))
        block.sync(mk("s"))


def build_program(T=2048, debug=False, dbg_k=0):
    NCH = T // TC
    nb = B_LOC
    nc = bass.Bass("TRN2", target_bir_lowering=False)

    x_hbm = nc.dram_tensor("x", [nb, D, T], F32, kind="ExternalInput").ap()
    xo_hbm = nc.dram_tensor("xo", [nb, D, T], F32, kind="ExternalOutput").ap()
    A_log = nc.dram_tensor("A_log", [D, N], F32, kind="ExternalInput").ap()
    Dskip_h = nc.dram_tensor("Dskip", [D], F32, kind="ExternalInput").ap()
    Wd_h = nc.dram_tensor("Wd", [D, D], F32, kind="ExternalInput").ap()
    bd_h = nc.dram_tensor("bd", [1, D], F32, kind="ExternalInput").ap()
    Wb_h = nc.dram_tensor("Wb", [D, N], F32, kind="ExternalInput").ap()
    bb_h = nc.dram_tensor("bb", [1, N], F32, kind="ExternalInput").ap()
    Wc_h = nc.dram_tensor("Wc", [D, N], F32, kind="ExternalInput").ap()
    bc_h = nc.dram_tensor("bc", [1, N], F32, kind="ExternalInput").ap()
    w1_h = nc.dram_tensor("w1", [H, D], F32, kind="ExternalInput").ap()
    cw_h = nc.dram_tensor("cw", [H, 3], F32, kind="ExternalInput").ap()
    cb_h = nc.dram_tensor("cb", [H], F32, kind="ExternalInput").ap()
    w2_h = nc.dram_tensor("w2", [D, H], F32, kind="ExternalInput").ap()
    ident_h = nc.dram_tensor("ident", [128, 128], F32, kind="ExternalInput").ap()

    dbg_h = nc.dram_tensor("dbg", [26, 128, 4096], F32, kind="ExternalOutput").ap() if debug else None
    bc_scr = nc.dram_tensor("bc_scr", [2, N, TC], F32, kind="Internal").ap()
    c_scr = nc.dram_tensor("c_scr", [N, TC], BF16, kind="Internal").ap()
    w1slab = nc.dram_tensor("w1slab", [NDT, H, 128], BF16, kind="Internal").ap()
    w1ts = nc.dram_tensor("w1ts", [NHT, 128, NDT, 128], BF16, kind="Internal").ap()
    w2q_d = nc.dram_tensor("w2q_d", [D, H], BF16, kind="Internal").ap()
    w2ts = nc.dram_tensor("w2ts", [NDT, 128, NHT, 128], BF16, kind="Internal").ap()

    stack = ExitStack()
    e = stack.enter_context
    S = Sched(nc, stack)

    xb = [[e(nc.sbuf_tensor(f"xb{s}_{d}", [128, TC], F32)) for d in range(NDT)] for s in range(3)]
    ubig = e(nc.sbuf_tensor("ubig", [128, NDT * TC], F32))
    ub = [ubig[:, d * TC:(d + 1) * TC] for d in range(NDT)]
    ubf = [[e(nc.sbuf_tensor(f"ubf{u}_{d}", [128, TC], BF16)) for d in range(NDT)] for u in range(2)]
    delta = [e(nc.sbuf_tensor(f"dl{d}", [128, TC], F32)) for d in range(NDT)]
    scanA = e(nc.sbuf_tensor("scanA", [128, N * TC], F32))
    scanB = e(nc.sbuf_tensor("scanB", [128, N * TC], F32))
    scanH = e(nc.sbuf_tensor("scanH", [128, N * TC], BF16))
    Brep = e(nc.sbuf_tensor("Brep", [128, N * TC], F32))
    Crep = e(nc.sbuf_tensor("Crep", [128, N * TC], BF16))
    hs_x = e(nc.sbuf_tensor("hs_x", [128, 516], F32))
    hqd = e(nc.sbuf_tensor("hqd", [128, NHT * 258], BF16, is_transpose_dest=True))
    hcarry = [e(nc.sbuf_tensor(f"hc{d}", [128, N], F32)) for d in range(NDT)]
    injb = e(nc.sbuf_tensor("injb", [128, N], F32))
    scr1 = e(nc.sbuf_tensor("scr1", [128, TC], F32))
    scr2 = e(nc.sbuf_tensor("scr2", [128, TC], F32))
    scr2b = e(nc.sbuf_tensor("scr2b", [128, TC], F32))
    scr3 = e(nc.sbuf_tensor("scr3", [128, 900], F32))
    Bsb = e(nc.sbuf_tensor("Bsb", [N, TC], F32))
    Csb = e(nc.sbuf_tensor("Csb", [N, TC], BF16))
    rows = e(nc.sbuf_tensor("rows", [1, 1024], F32))
    mr_rep = e(nc.sbuf_tensor("mr_rep", [128, 512], F32))
    s1_rep = e(nc.sbuf_tensor("s1_rep", [128, 512], F32))
    s2_rep = e(nc.sbuf_tensor("s2_rep", [128, 384], F32))
    si2_rep = e(nc.sbuf_tensor("si2_rep", [128, 384], F32))
    tmax = e(nc.sbuf_tensor("tmax", [128, 384], F32))
    scol = e(nc.sbuf_tensor("scol", [128, 16], F32))
    srow = e(nc.sbuf_tensor("srow", [1, 768], F32))
    hs = []
    for i in range(15):
        hs.append(scanA[:, i * 258:i * 258 + 257])
    for i in range(15):
        hs.append(scanB[:, i * 258:i * 258 + 257])
    hs.append(hs_x[:, 0:257])
    hs.append(hs_x[:, 258:515])
    hq = [hqd[:, i * 258:i * 258 + 257] for i in range(NHT)]
    hC_bf = scanA[:].bitcast(BF16)
    h1buf = [e(nc.sbuf_tensor(f"h1b{i}", [128, 259], F32)) for i in range(2)]
    h1r = [e(nc.sbuf_tensor(f"h1r{i}", [128, TC], F32)) for i in range(2)]
    f2s = [e(nc.sbuf_tensor(f"f2s{i}", [128, 257], F32)) for i in range(NDT)]
    stash = e(nc.sbuf_tensor("stash", [128, 2 * NHT], F32))
    w1c = [e(nc.sbuf_tensor(f"w1c{i}", [128, 8 * 128], BF16)) for i in range(3)]
    w2c = [e(nc.sbuf_tensor(f"w2c{i}", [128, H], BF16)) for i in range(2)]
    Wd_bf = [e(nc.sbuf_tensor(f"Wdb{d}", [128, D], BF16)) for d in range(NDT)]
    Wb_bf = [e(nc.sbuf_tensor(f"Wbb{d}", [128, N], BF16)) for d in range(NDT)]
    Wc_bf = [e(nc.sbuf_tensor(f"Wcb{d}", [128, N], BF16)) for d in range(NDT)]
    A_sb = e(nc.sbuf_tensor("A_sb", [128, NDT * N], F32))
    Dsk_sb = e(nc.sbuf_tensor("Dsk_sb", [128, NDT], F32))
    convc = e(nc.sbuf_tensor("convc", [128, 4 * NHT], F32))
    bdc = e(nc.sbuf_tensor("bdc", [128, NDT], F32))
    bbc_c = e(nc.sbuf_tensor("bbc_c", [N, 2], F32))
    ident = e(nc.sbuf_tensor("ident_sb", [128, 128], F32))
    ones_col = e(nc.sbuf_tensor("ones_col", [128, 1], F32))
    ones_rowf = e(nc.sbuf_tensor("ones_rowf", [1, 128], F32))
    sw_sc = e(nc.sbuf_tensor("sw_sc", [1, 8], F32))
    swcol = e(nc.sbuf_tensor("swcol", [128, 2], F32))
    smallw = e(nc.sbuf_tensor("smallw", [128, 16], F32))
    cst = e(nc.sbuf_tensor("cst", [1, 4], F32))
    wq_buf = hqd[:, 0:H]

    ps = [e(nc.psum_tensor(f"ps{i}", [128, 512], F32)) for i in range(8)]
    PS_LN0, PS_LN1, PS_PRJ0, PS_PRJ1, PS_F1A, PS_F1B, PS_F2A, PS_F2B = range(8)

    V, A_, T_, G, SY = "v", "a", "t", "g", "s"

    dbg_used = [0]

    def dbg_dump(idx, ap, cols, rows=128):
        if not debug:
            return
        S.need(G, V)
        S.need(G, A_)
        S.need(G, T_)
        c = S.dma(G, dbg_h[idx, 0:rows, 0:cols], ap)
        S.wait_dma(V, G, c)
        S.wait_dma(A_, G, c)
        S.wait_dma(T_, G, c)
        dbg_used[0] = max(dbg_used[0], idx + 1)

    S.emit(V, lambda E: E.memset(ones_col[:], 1.0))
    S.emit(V, lambda E: E.memset(ones_rowf[:], 1.0))
    S.emit(V, lambda E: E.memset(cst[:, 0:1], 1e-5))
    S.emit(V, lambda E: E.memset(cst[:, 1:2], -0.5))
    S.emit(V, lambda E: E.memset(h1buf[0][:, 258:259], 0.0))
    S.emit(V, lambda E: E.memset(h1buf[1][:, 258:259], 0.0))
    S.dma(SY, ident[:], ident_h)

    for d in range(NDT):
        c = S.dma(SY, scanA[:, 0:D], Wd_h[d * 128:(d + 1) * 128, :])
        S.wait_dma(V, SY, c)
        S.emit(V, lambda E, d=d: E.tensor_copy(out=Wd_bf[d][:], in_=scanA[:, 0:D]))
        S.wait(SY, S.sem[V], S.milestone(V))
    for d in range(NDT):
        c = S.dma(SY, scanA[:, 0:N], Wb_h[d * 128:(d + 1) * 128, :])
        c = S.dma(SY, scanA[:, N:2 * N], Wc_h[d * 128:(d + 1) * 128, :])
        c = S.dma(SY, scanA[:, 2 * N:3 * N], A_log[d * 128:(d + 1) * 128, :])
        S.wait_dma(V, SY, c)
        S.emit(V, lambda E, d=d: E.tensor_copy(out=Wb_bf[d][:], in_=scanA[:, 0:N]))
        S.emit(V, lambda E, d=d: E.tensor_copy(out=Wc_bf[d][:], in_=scanA[:, N:2 * N]))
        S.need(A_, V)
        S.emit(A_, lambda E: E.activation(out=scanA[:, 3 * N:4 * N], in_=scanA[:, 2 * N:3 * N], func=AF.Exp))
        S.need(V, A_)
        S.emit(V, lambda E, d=d: E.tensor_scalar(out=A_sb[:, d * N:(d + 1) * N], in0=scanA[:, 3 * N:4 * N],
                                                 scalar1=-1.0, scalar2=None, op0=ALU.mult))
        S.vguard()
        S.wait(SY, S.sem[V], S.milestone(V))
    for d in range(NDT):
        S.dma(SY, Dsk_sb[:, d:d + 1], Dskip_h[d * 128:(d + 1) * 128].rearrange("(p a) -> p a", a=1),
              allow_slow_non_contiguous=True)
    for et in range(NDT):
        S.dma(SY, bdc[:, et:et + 1], bd_h[0:1, et * 128:(et + 1) * 128].rearrange("a p -> p a"),
              allow_slow_non_contiguous=True)
    S.dma(SY, bbc_c[:, 0:1], bb_h.rearrange("a p -> p a"), allow_slow_non_contiguous=True)
    S.dma(SY, bbc_c[:, 1:2], bc_h.rearrange("a p -> p a"), allow_slow_non_contiguous=True)
    for i in range(NHT):
        c = S.dma(SY, smallw[:, 0:3], cw_h[i * 128:(i + 1) * 128, :])
        c = S.dma(SY, convc[:, 3 * NHT + i:3 * NHT + i + 1],
                  cb_h[i * 128:(i + 1) * 128].rearrange("(p a) -> p a", a=1),
                  allow_slow_non_contiguous=True)
        S.wait_dma(V, SY, c)
        S.emit(V, lambda E: E.tensor_scalar(out=smallw[:, 4:5], in0=smallw[:, 1:2], scalar1=-1.0, scalar2=None, op0=ALU.mult))
        S.vguard()
        S.emit(V, lambda E: E.tensor_tensor(out=smallw[:, 4:5], in0=smallw[:, 4:5], in1=smallw[:, 1:2], op=ALU.max))
        S.vguard()
        S.emit(V, lambda E: E.tensor_scalar(out=smallw[:, 5:6], in0=smallw[:, 4:5], scalar1=1e-20, scalar2=None, op0=ALU.is_lt))
        S.vguard()
        S.emit(V, lambda E: E.tensor_scalar(out=smallw[:, 6:7], in0=smallw[:, 5:6], scalar1=-1.0, scalar2=1.0, op0=ALU.mult, op1=ALU.add))
        S.vguard()
        S.emit(V, lambda E: E.tensor_tensor(out=smallw[:, 7:8], in0=smallw[:, 1:2], in1=smallw[:, 6:7], op=ALU.mult))
        S.vguard()
        S.emit(V, lambda E: E.tensor_scalar(out=smallw[:, 8:9], in0=smallw[:, 5:6], scalar1=1e-20, scalar2=None, op0=ALU.mult))
        S.vguard()
        S.emit(V, lambda E, i=i: E.tensor_tensor(out=convc[:, 2 * NHT + i:2 * NHT + i + 1], in0=smallw[:, 7:8], in1=smallw[:, 8:9], op=ALU.add))
        S.vguard()
        S.emit(V, lambda E, i=i: E.reciprocal(out=smallw[:, 9:10], in_=convc[:, 2 * NHT + i:2 * NHT + i + 1]))
        S.vguard()
        S.emit(V, lambda E, i=i: E.tensor_tensor(out=convc[:, i:i + 1], in0=smallw[:, 0:1], in1=smallw[:, 9:10], op=ALU.mult))
        S.vguard()
        S.emit(V, lambda E, i=i: E.tensor_tensor(out=convc[:, NHT + i:NHT + i + 1], in0=smallw[:, 2:3], in1=smallw[:, 9:10], op=ALU.mult))
        S.vguard()
        S.wait(SY, S.sem[V], S.milestone(V))

    def sy_sync_compute():
        for eng in (V, A_, T_):
            c = S.milestone(eng)
            S.wait(SY, S.sem[eng], c)

    def wsum(w_src, n_tiles, tile_cols, mean_div, out_slot):
        nmm = n_tiles * (tile_cols // 512)
        k = [0]
        stg = [scanA, Brep]
        toks = {}
        a_ms = {}
        t_ms = {}
        sy_sync_compute()
        for i in range(min(2, n_tiles)):
            toks[i] = S.dma(SY, stg[i % 2][:, 0:tile_cols], w_src[i * 128:(i + 1) * 128, :])
        for i in range(n_tiles):
            S.wait_dma(A_, SY, toks[i])
            S.emit(A_, lambda E, i=i, tc=tile_cols: E.activation(out=scanB[:, 0:tc], in_=stg[i % 2][:, 0:tc], func=AF.Abs))
            a_ms[i] = S.milestone(A_)
            if i + 2 < n_tiles:
                S.wait(SY, S.sem[A_], a_ms[i])
                toks[i + 2] = S.dma(SY, stg[i % 2][:, 0:tile_cols], w_src[(i + 2) * 128:(i + 3) * 128, :])
            S.wait(T_, S.sem[A_], a_ms[i])
            for j in range(0, tile_cols, 512):
                st = (k[0] == 0)
                sp = (k[0] == nmm - 1)
                k[0] += 1
                S.emit(T_, lambda E, j=j, st=st, sp=sp: E.matmul(out=ps[PS_LN0][0:1, 0:512], lhsT=ones_col[:],
                                                                 rhs=scanB[:, j:j + 512], start=st, stop=sp))
            t_ms[i] = S.milestone(T_)
            S.wait(A_, S.sem[T_], t_ms[i])
        S.need(V, T_)
        S.emit(V, lambda E: E.tensor_copy(out=rows[:, 0:512], in_=ps[PS_LN0][0:1, 0:512]))
        S.vguard()
        S.emit(V, lambda E: E.reduce_sum(out=rows[:, 512:513], in_=rows[:, 0:512], axis=AX.X))
        S.vguard()
        S.emit(V, lambda E, md=mean_div: E.tensor_scalar(out=rows[:, 513:514], in0=rows[:, 512:513],
                                                         scalar1=1.0 / md, scalar2=1e-5, op0=ALU.mult, op1=ALU.max))
        S.vguard()
        S.emit(V, lambda E, o=out_slot: E.tensor_copy(out=sw_sc[:, o + 1:o + 2], in_=rows[:, 513:514]))
        S.vguard()
        S.emit(V, lambda E, o=out_slot: E.reciprocal(out=sw_sc[:, o:o + 1], in_=rows[:, 513:514]))
        S.vguard()

    wsum(w1_h, NHT, D, H * D, 0)
    wsum(w2_h, NDT, H, H * D, 2)
    S.need(T_, V)
    S.emit(T_, lambda E: E.matmul(out=ps[PS_LN1][:, 0:8], lhsT=ones_rowf[:], rhs=sw_sc[:, 0:8], start=True, stop=True))
    S.need(V, T_)
    S.emit(V, lambda E: E.tensor_copy(out=swcol[:, 0:1], in_=ps[PS_LN1][:, 1:2]))
    S.vguard()
    S.emit(V, lambda E: E.tensor_copy(out=swcol[:, 1:2], in_=ps[PS_LN1][:, 3:4]))
    S.vguard()
    S.emit(V, lambda E: E.tensor_copy(out=smallw[:, 10:11], in_=ps[PS_LN1][:, 0:1]))
    S.vguard()
    S.emit(V, lambda E: E.tensor_copy(out=smallw[:, 11:12], in_=ps[PS_LN1][:, 2:3]))
    S.vguard()

    qv = wq_buf
    ldt = {}
    stt_ = {}
    vms = {}
    sy_sync_compute()
    for i in range(4):
        ldt[i] = S.dma(SY, scanA[:, (i % 4) * D:(i % 4 + 1) * D], w1_h[i * 128:(i + 1) * 128, :])
    for i in range(NHT):
        j = i % 4
        S.wait_dma(V, SY, ldt[i])
        if i >= 4 and (i - 4) in stt_:
            S.wait_dma(V, stt_[i - 4])
        S.emit(V, lambda E, j=j: E.tensor_scalar(out=scanB[:, j * D:(j + 1) * D], in0=scanA[:, j * D:(j + 1) * D],
                                            scalar1=smallw[:, 10:11], scalar2=MAGIC, op0=ALU.mult, op1=ALU.add))
        S.emit(V, lambda E, j=j: E.tensor_scalar(out=scanB[:, j * D:(j + 1) * D], in0=scanB[:, j * D:(j + 1) * D],
                                            scalar1=MAGIC, scalar2=1.0, op0=ALU.subtract, op1=ALU.min))
        S.emit(V, lambda E, j=j: E.tensor_scalar(out=qv[:, j * D:(j + 1) * D], in0=scanB[:, j * D:(j + 1) * D],
                                            scalar1=-1.0, scalar2=None, op0=ALU.max))
        vms[i] = S.milestone(V)
        S.wait(SY, S.sem[V], vms[i])
        stt_[i] = S.dma(SY, w1slab[:, i * 128:(i + 1) * 128, :].rearrange("ds p c -> p ds c"),
                        qv[:, j * D:(j + 1) * D])
        if i + 4 < NHT:
            ldt[i + 4] = S.dma(SY, scanA[:, j * D:(j + 1) * D], w1_h[(i + 4) * 128:(i + 5) * 128, :])
    cpre = S.dcnt["s"]
    S.wait_dma(SY, SY, cpre)
    for ds in range(NDT):
        c = S.dma(SY, qv[:, 0:H], w1slab[ds], transpose=True)
        S.wait_dma(SY, SY, c)
        c = S.dma(SY, w1ts[:, :, ds, :].rearrange("ht p c -> p ht c"),
                  qv[:, 0:H])
        S.wait_dma(SY, SY, c)
    stg2 = [scanA, Brep]
    out2 = [qv, hqd[:, H:2 * H]]
    ld2 = {}
    st2 = {}
    vm2 = {}
    sy_sync_compute()
    for i in range(min(2, NDT)):
        ld2[i] = S.dma(SY, stg2[i % 2][:, 0:H], w2_h[i * 128:(i + 1) * 128, :])
    for i in range(NDT):
        j = i % 2
        S.wait_dma(V, SY, ld2[i])
        if (i - 2) in st2:
            S.wait_dma(V, st2[i - 2])
        S.emit(V, lambda E, j=j: E.tensor_scalar(out=scanB[:, 0:H], in0=stg2[j][:, 0:H],
                                            scalar1=smallw[:, 11:12], scalar2=MAGIC, op0=ALU.mult, op1=ALU.add))
        S.emit(V, lambda E: E.tensor_scalar(out=scanB[:, 0:H], in0=scanB[:, 0:H],
                                            scalar1=MAGIC, scalar2=1.0, op0=ALU.subtract, op1=ALU.min))
        S.emit(V, lambda E, j=j: E.tensor_scalar(out=out2[j][:, 0:H] if j == 0 else out2[j], in0=scanB[:, 0:H],
                                            scalar1=-1.0, scalar2=None, op0=ALU.max))
        vm2[i] = S.milestone(V)
        S.wait(SY, S.sem[V], vm2[i])
        st2[i] = S.dma(SY, w2q_d[i * 128:(i + 1) * 128, :], out2[j][:, 0:H] if j == 0 else out2[j])
        if i + 2 < NDT:
            ld2[i + 2] = S.dma(SY, stg2[j][:, 0:H], w2_h[(i + 2) * 128:(i + 3) * 128, :])
    S.wait_dma(SY, st2[NDT - 1])
    for ht in range(NHT):
        c = S.dma(SY, qv[:, 0:D], w2q_d[:, ht * 128:(ht + 1) * 128], transpose=True)
        S.wait_dma(SY, SY, c)
        c = S.dma(SY, w2ts[:, :, ht, :].rearrange("ds p c -> p ds c"),
                  qv[:, 0:D])
        S.wait_dma(SY, SY, c)

    for eng in (V, A_, T_):
        S.wait_dma(eng, SY, S.dcnt["s"])
    S.fence_all()

    xout_counts = {}
    vq_ms = {}
    brep_free_v = [0]

    def bank_guard(engine_from, bank_state, bank):
        eng, cn = bank_state.get(bank, (None, 0))
        if eng is not None:
            S.wait(engine_from, S.sem[eng], cn)

    bank_state = {}

    def ln_sums(src_tiles, bank_a, bank_b):
        S.need(A_, T_)
        bank_guard(T_, bank_state, bank_a)
        bank_guard(T_, bank_state, bank_b)
        for dt in range(NDT):
            S.emit(A_, lambda E, dt=dt: E.activation(out=scr2b[:], in_=src_tiles[dt][:], func=AF.Square))
            S.need(T_, A_)
            S.emit(T_, lambda E, dt=dt, bank_a=bank_a: E.matmul(out=ps[bank_a][0:1, 0:TC], lhsT=ones_col[:],
                                                 rhs=src_tiles[dt][:], start=(dt == 0), stop=(dt == NDT - 1)))
            S.emit(T_, lambda E, dt=dt, bank_b=bank_b: E.matmul(out=ps[bank_b][0:1, 0:TC], lhsT=ones_col[:],
                                                 rhs=scr2b[:], start=(dt == 0), stop=(dt == NDT - 1)))
            S.need(A_, T_)
        tms = S.milestone(T_)
        return tms

    def ln_rest(src_tiles, dst_tiles, also_bf, bank_a, bank_b, tms):
        S.wait(V, S.sem[T_], tms)
        S.emit(V, lambda E, bank_a=bank_a: E.tensor_copy(out=rows[:, 0:TC], in_=ps[bank_a][0:1, 0:TC]))
        S.vguard()
        S.emit(V, lambda E, bank_b=bank_b: E.tensor_copy(out=rows[:, 256:256 + TC], in_=ps[bank_b][0:1, 0:TC]))
        S.vguard()
        bank_state[bank_a] = (V, S.milestone(V))
        bank_state[bank_b] = (V, S.cnt[V])
        S.emit(V, lambda E: E.tensor_scalar(out=rows[:, 512:768], in0=rows[:, 0:256], scalar1=1.0 / D, scalar2=None, op0=ALU.mult))
        S.vguard()
        S.emit(V, lambda E: E.tensor_scalar(out=rows[:, 256:512], in0=rows[:, 256:512], scalar1=1.0 / D, scalar2=None, op0=ALU.mult))
        S.vguard()
        S.emit(V, lambda E: E.tensor_tensor(out=rows[:, 0:256], in0=rows[:, 512:768], in1=rows[:, 512:768], op=ALU.mult))
        S.vguard()
        S.emit(V, lambda E: E.tensor_tensor(out=rows[:, 0:256], in0=rows[:, 256:512], in1=rows[:, 0:256], op=ALU.subtract))
        S.vguard()
        S.need(A_, V)
        S.emit(A_, lambda E: E.activation(out=rows[:, 256:512], in_=rows[:, 0:256], func=AF.Ln, bias=cst[:, 0:1]))
        S.vguard("a")
        S.emit(A_, lambda E: E.activation(out=rows[:, 768:1024], in_=rows[:, 256:512], func=AF.Exp, scale=cst[:, 1:2]))
        S.need(T_, A_)
        bank_guard(T_, bank_state, bank_a)
        S.emit(T_, lambda E, bank_a=bank_a: E.matmul(out=ps[bank_a][:, 0:512], lhsT=ones_rowf[:], rhs=rows[:, 512:1024], start=True, stop=True))
        S.need(V, T_)
        S.emit(V, lambda E, bank_a=bank_a: E.tensor_copy(out=mr_rep[:], in_=ps[bank_a][:, 0:512]))
        S.vguard()
        bank_state[bank_a] = (V, S.milestone(V))
        for dt in range(NDT):
            S.emit(V, lambda E, dt=dt: E.tensor_tensor(out=dst_tiles[dt], in0=src_tiles[dt][:], in1=mr_rep[:, 0:256], op=ALU.subtract))
            S.emit(V, lambda E, dt=dt: E.tensor_tensor(out=dst_tiles[dt], in0=dst_tiles[dt], in1=mr_rep[:, 256:512], op=ALU.mult))
            S.vguard()
            if also_bf is not None:
                S.emit(V, lambda E, dt=dt: E.tensor_copy(out=also_bf[dt][:], in_=dst_tiles[dt]))

    def quant_cols(n_chunks, sw_col_idx, srow_out, src_tiles_or_tmax, L, dbg_on=False):
        S.need(T_, V)
        bank_guard(T_, bank_state, PS_PRJ0)
        bank_guard(T_, bank_state, PS_PRJ1)
        for j in range(n_chunks):
            S.emit(T_, lambda E, j=j: E.transpose(out=ps[PS_PRJ0][:, j * 128:(j + 1) * 128],
                                                  in_=tmax[:, j * 128:(j + 1) * 128], identity=ident[:]))
        S.need(V, T_)
        for j in range(n_chunks):
            S.emit(V, lambda E, j=j: E.reduce_max(out=scol[:, j:j + 1], in_=ps[PS_PRJ0][:, j * 128:(j + 1) * 128],
                                                  axis=AX.X, apply_absolute_value=True))
            S.vguard()
        bank_state[PS_PRJ0] = (V, S.milestone(V))
        nch = n_chunks
        S.emit(V, lambda E: E.tensor_scalar(out=scol[:, 4:4 + nch], in0=scol[:, 0:nch], scalar1=1e-5, scalar2=None, op0=ALU.max))
        S.vguard()
        S.emit(V, lambda E: E.reciprocal(out=scol[:, 8:8 + nch], in_=scol[:, 4:4 + nch]))
        S.vguard()
        S.emit(V, lambda E: E.tensor_scalar(out=scol[:, 8:8 + nch], in0=scol[:, 8:8 + nch], scalar1=127.0, scalar2=None, op0=ALU.mult))
        S.vguard()
        S.emit(V, lambda E, si=sw_col_idx: E.tensor_scalar(out=scol[:, 12:12 + nch], in0=scol[:, 4:4 + nch],
                                                           scalar1=1.0 / 127.0, scalar2=swcol[:, si:si + 1],
                                                           op0=ALU.mult, op1=ALU.mult))
        S.vguard()
        S.need(T_, V)
        bank_guard(T_, bank_state, PS_PRJ0)
        for j in range(nch):
            S.emit(T_, lambda E, j=j: E.transpose(out=ps[PS_PRJ0][0:1, j * 128:(j + 1) * 128],
                                                  in_=scol[:, 8 + j:9 + j], identity=ident[:]))
        for j in range(nch):
            S.emit(T_, lambda E, j=j, o=384: E.transpose(out=ps[PS_PRJ1][0:1, j * 128:(j + 1) * 128],
                                                         in_=scol[:, 12 + j:13 + j], identity=ident[:]))
        S.need(V, T_)
        S.emit(V, lambda E, n=nch: E.tensor_copy(out=srow[:, 0:n * 128], in_=ps[PS_PRJ0][0:1, 0:n * 128]))
        S.vguard()
        S.emit(V, lambda E, n=nch: E.tensor_copy(out=srow[:, 384:384 + n * 128], in_=ps[PS_PRJ1][0:1, 0:n * 128]))
        S.vguard()
        bank_state[PS_PRJ0] = (V, S.milestone(V))
        bank_state[PS_PRJ1] = (V, S.cnt[V])
        if dbg_on:
            dbg_dump(8, scol[:], 16)
            dbg_dump(13, srow[:], 768, rows=1)

    def front_a(b, k, g):
        slot = g % 3
        if k == 0:
            for dt in range(NDT):
                S.emit(V, lambda E, dt=dt: E.memset(hcarry[dt][:], 0.0))
            S.vguard()
        if (g - 3) in xout_counts:
            S.wait_dma(SY, xout_counts[g - 3])
        cx = 0
        for dt in range(NDT):
            cx = S.dma(SY, xb[slot][dt][:], x_hbm[b, dt * 128:(dt + 1) * 128, k * TC:(k + 1) * TC])
        S.wait_dma(V, SY, cx)
        S.wait_dma(A_, SY, cx)
        S.wait_dma(T_, SY, cx)
        tms = ln_sums(xb[slot], PS_F2A, PS_F2B)
        return tms

    def mixer_front(b, k, g, tms):
        slot = g % 3
        uslot = k % 2
        delta_ms = {}
        ln_rest(xb[slot], ub, ubf[uslot], PS_F2A, PS_F2B, tms)
        if b == 0 and k == dbg_k:
            dbg_dump(0, ubig[:, 0:2048], 2048)
            dbg_dump(11, mr_rep[:], 512)
        S.need(T_, V)
        bank_guard(T_, bank_state, PS_PRJ1)
        for dk in range(NDT):
            S.emit(T_, lambda E, dk=dk, uslot=uslot: E.matmul(out=ps[PS_PRJ1][0:N, 0:TC], lhsT=Wb_bf[dk][:],
                                                 rhs=ubf[uslot][dk][:], start=(dk == 0), stop=(dk == NDT - 1)))
        for dk in range(NDT):
            S.emit(T_, lambda E, dk=dk, uslot=uslot: E.matmul(out=ps[PS_PRJ1][0:N, 256:256 + TC], lhsT=Wc_bf[dk][:],
                                                 rhs=ubf[uslot][dk][:], start=(dk == 0), stop=(dk == NDT - 1)))
        S.need(V, T_)
        S.emit(V, lambda E: E.tensor_scalar(out=Bsb[:], in0=ps[PS_PRJ1][0:N, 0:TC],
                                            scalar1=bbc_c[:, 0:1], scalar2=None, op0=ALU.add))
        S.emit(V, lambda E: E.tensor_scalar(out=Csb[:], in0=ps[PS_PRJ1][0:N, 256:256 + TC],
                                            scalar1=bbc_c[:, 1:2], scalar2=None, op0=ALU.add))
        S.vguard()
        bank_state[PS_PRJ1] = (V, S.milestone(V))
        S.wait("a", S.sem[V], S.cnt[V])
        cbo = S.dma("a", bc_scr[0], Bsb[:])
        cbo = S.dma("a", c_scr, Csb[:])
        S.wait_dma("a", "a", cbo)
        S.wait(A_, S.sem[V], brep_free_v[0])
        cbi = S.dma("a", Brep[:], bc_scr[0].rearrange("n t -> (n t)").unsqueeze(0).partition_broadcast(128))
        cbi = S.dma("a", Crep[:], c_scr.rearrange("n t -> (n t)").unsqueeze(0).partition_broadcast(128))
        for et in range(NDT):
            bank = PS_PRJ0 if et % 2 == 0 else PS_PRJ1
            bank_guard(T_, bank_state, bank)
            for dk in range(NDT):
                S.emit(T_, lambda E, dk=dk, et=et, bank=bank, uslot=uslot: E.matmul(
                    out=ps[bank][:, 0:TC], lhsT=Wd_bf[dk][:, et * 128:(et + 1) * 128],
                    rhs=ubf[uslot][dk][:], start=(dk == 0), stop=(dk == NDT - 1)))
            S.need(A_, T_)
            S.emit(A_, lambda E, bank=bank, et=et: E.activation(out=scr2[:], in_=ps[bank][:, 0:TC],
                                                               func=AF.Exp, bias=bdc[:, et:et + 1]))
            S.emit(A_, lambda E, et=et: E.activation(out=delta[et][:], in_=scr2[:], func=AF.Ln, bias=ones_col[:]))
            delta_ms[et] = S.milestone(A_)
            bank_state[bank] = (A_, delta_ms[et])
        return cbi, delta_ms

    def mixer_back(b, k, g, ctx):
        slot = g % 3
        uslot = k % 2
        cbi, delta_ms = ctx
        S.wait_dma(V, "a", cbi)
        if b == 0 and k == dbg_k:
            dbg_dump(1, delta[0][:], 256)
            dbg_dump(2, delta[7][:], 256)
            dbg_dump(9, Bsb[:], 256, rows=16)
            dbg_dump(10, Brep[:], 4096)
        for dt in range(NDT):
            S.wait(V, S.sem[A_], delta_ms[dt])
            S.emit(V, lambda E, dt=dt: E.tensor_tensor(out=scr1[:], in0=delta[dt][:], in1=ub[dt], op=ALU.mult))
            for n in range(8):
                S.emit(V, lambda E, dt=dt, n=n: E.tensor_scalar(
                    out=scanA[:, n * TC:(n + 1) * TC], in0=delta[dt][:],
                    scalar1=A_sb[:, dt * N + n:dt * N + n + 1], scalar2=None, op0=ALU.mult))
            S.need(A_, V)
            S.emit(A_, lambda E: E.activation(out=scanA[:, 0:8 * TC], in_=scanA[:, 0:8 * TC], func=AF.Exp))
            exp_ms1 = S.milestone(A_)
            for n in range(8, N):
                S.emit(V, lambda E, dt=dt, n=n: E.tensor_scalar(
                    out=scanA[:, n * TC:(n + 1) * TC], in0=delta[dt][:],
                    scalar1=A_sb[:, dt * N + n:dt * N + n + 1], scalar2=None, op0=ALU.mult))
            S.need(A_, V)
            S.emit(A_, lambda E: E.activation(out=scanA[:, 8 * TC:16 * TC], in_=scanA[:, 8 * TC:16 * TC], func=AF.Exp))
            exp_ms2 = S.milestone(A_)
            S.wait(V, S.sem[A_], exp_ms1)
            S.emit(V, lambda E: E.tensor_tensor(
                out=scanB[:, 0:8 * TC].rearrange("p (n t) -> p n t", n=8),
                in0=scr1[:].unsqueeze(1).broadcast_to([128, 8, TC]),
                in1=Brep[:, 0:8 * TC].rearrange("p (n t) -> p n t", n=8), op=ALU.mult))
            S.wait(V, S.sem[A_], exp_ms2)
            S.emit(V, lambda E: E.tensor_tensor(
                out=scanB[:, 8 * TC:16 * TC].rearrange("p (n t) -> p n t", n=8),
                in0=scr1[:].unsqueeze(1).broadcast_to([128, 8, TC]),
                in1=Brep[:, 8 * TC:16 * TC].rearrange("p (n t) -> p n t", n=8), op=ALU.mult))
            S.vguard()
            if b == 0 and k == dbg_k and dt == 0:
                dbg_dump(2, hcarry[0][:], N)
            S.emit(V, lambda E, dt=dt: E.tensor_tensor(out=injb[:], in0=scanA[:, 0::TC], in1=hcarry[dt][:], op=ALU.mult))
            S.vguard()
            S.emit(V, lambda E: E.tensor_tensor(out=scanB[:, 0::TC], in0=scanB[:, 0::TC], in1=injb[:], op=ALU.add))
            S.vguard()
            S.emit(V, lambda E: E.memset(scanA[:, 0::TC], 0.0))
            S.vguard()
            if b == 0 and k == dbg_k and dt == 0:
                dbg_dump(3, scanA[:], 4096)
                dbg_dump(4, scanB[:], 4096)
            S.emit(V, lambda E: E.tensor_tensor_scan(out=scanH[:], data0=scanA[:], data1=scanB[:],
                                                     initial=0.0, op0=ALU.mult, op1=ALU.add))
            S.vguard()
            if b == 0 and k == dbg_k and dt == 0:
                dbg_dump(5, scanH[:], 4096)
            S.emit(V, lambda E, dt=dt: E.tensor_copy(out=hcarry[dt][:], in_=scanH[:, TC - 1::TC]))
            S.emit(V, lambda E: E.tensor_tensor(out=hC_bf[:, 0:4096], in0=scanH[:], in1=Crep[:], op=ALU.mult))
            S.emit(V, lambda E: E.tensor_tensor(out=hC_bf[:, 0:8 * TC], in0=hC_bf[:, 0:8 * TC], in1=hC_bf[:, 8 * TC:16 * TC], op=ALU.add))
            S.emit(V, lambda E: E.tensor_tensor(out=hC_bf[:, 0:4 * TC], in0=hC_bf[:, 0:4 * TC], in1=hC_bf[:, 4 * TC:8 * TC], op=ALU.add))
            S.emit(V, lambda E: E.tensor_tensor(out=hC_bf[:, 0:2 * TC], in0=hC_bf[:, 0:2 * TC], in1=hC_bf[:, 2 * TC:4 * TC], op=ALU.add))
            S.emit(V, lambda E: E.tensor_tensor(out=hC_bf[:, 0:TC], in0=hC_bf[:, 0:TC], in1=hC_bf[:, TC:2 * TC], op=ALU.add))
            S.emit(V, lambda E, dt=dt: E.scalar_tensor_tensor(out=scr1[:], in0=ub[dt], scalar=Dsk_sb[:, dt:dt + 1],
                                                              in1=hC_bf[:, 0:TC], op0=ALU.mult, op1=ALU.add))
            S.emit(V, lambda E, dt=dt, slot=slot: E.tensor_tensor(out=xb[slot][dt][:], in0=xb[slot][dt][:], in1=scr1[:], op=ALU.add))
            if b == 0 and k == dbg_k and dt == 0:
                dbg_dump(6, scanA[:, 0:256], 256)
                dbg_dump(7, xb[slot][0][:], 256)
        brep_free_v[0] = S.milestone(V)
        tms2 = ln_sums(xb[slot], PS_LN0, PS_LN1)
        ln_rest(xb[slot], ub, None, PS_LN0, PS_LN1, tms2)
        S.emit(V, lambda E: E.reduce_max(
            out=tmax[:, 0:TC],
            in_=ubig[:].rearrange("p (dt t) -> p dt t", t=TC).transpose([0, 2, 1]),
            axis=AX.X, apply_absolute_value=True))
        quant_cols(2, 0, srow, tmax, TC, dbg_on=(debug and b == 0 and k == dbg_k))
        S.need(T_, V)
        bank_guard(T_, bank_state, PS_LN1)
        S.emit(T_, lambda E: E.matmul(out=ps[PS_LN1][:, 0:256], lhsT=ones_rowf[:], rhs=srow[:, 0:256], start=True, stop=True))
        S.emit(T_, lambda E: E.matmul(out=ps[PS_LN1][:, 256:512], lhsT=ones_rowf[:], rhs=srow[:, 384:640], start=True, stop=True))
        S.need(V, T_)
        S.emit(V, lambda E: E.tensor_copy(out=s1_rep[:], in_=ps[PS_LN1][:, 0:512]))
        bank_state[PS_LN1] = (V, S.milestone(V))
        if b == 0 and k == dbg_k:
            dbg_dump(12, s1_rep[:], 512)
        for dt in range(NDT):
            S.emit(V, lambda E, dt=dt: E.tensor_tensor(out=scr2[:], in0=ub[dt], in1=s1_rep[:, 0:256], op=ALU.mult))
            S.emit(V, lambda E, dt=dt, uslot=uslot: E.tensor_scalar(out=ubf[uslot][dt][:], in0=scr2[:], scalar1=MAGIC, scalar2=MAGIC,
                                                       op0=ALU.add, op1=ALU.subtract))
        vq_ms[g] = S.milestone(V)

    def ffn_p1(b, k, g):
        slot = g % 3
        pslot = (g - 1) % 3
        uslot = k % 2
        if k == 0:
            S.emit(V, lambda E: E.memset(stash[:], 0.0))
            S.vguard()
        if k == 0:
            tw0, L, ilo = 0, TC - 1, 2
        elif k == NCH - 1:
            tw0, L, ilo = k * TC - 1, TC + 1, 1
        else:
            tw0, L, ilo = k * TC - 1, TC, 1
        S.wait(T_, S.sem[V], vq_ms[g])
        S.emit(V, lambda E, L=L: E.memset(tmax[:, 0:384], 0.0))
        cw1 = {}
        t_ms1 = {}
        cw1[0] = S.dma(SY, w1c[0][:], w1ts[0].rearrange("p dk c -> p (dk c)"))
        cw1[1] = S.dma(SY, w1c[1][:], w1ts[1].rearrange("p dk c -> p (dk c)"))
        cw1[2] = S.dma(SY, w1c[2][:], w1ts[2].rearrange("p dk c -> p (dk c)"))
        for ht in range(NHT):
            pp3 = ht % 3
            pp = ht % 2
            S.wait_dma(T_, SY, cw1[ht])
            bank = PS_F1A if pp == 0 else PS_F1B
            bank_guard(T_, bank_state, bank)
            for dk in range(NDT):
                S.emit(T_, lambda E, dk=dk, pp3=pp3, bank=bank, uslot=uslot: E.matmul(
                    out=ps[bank][:, 0:TC], lhsT=w1c[pp3][:, dk * 128:(dk + 1) * 128],
                    rhs=ubf[uslot][dk][:], start=(dk == 0), stop=(dk == NDT - 1)))
            t_ms1[ht] = S.milestone(T_)
            if ht + 3 < NHT:
                S.wait(SY, S.sem[T_], t_ms1[ht])
                cw1[ht + 3] = S.dma(SY, w1c[(ht + 3) % 3][:],
                                    w1ts[ht + 3].rearrange("p dk c -> p (dk c)"))
            S.wait(A_, S.sem[T_], t_ms1[ht])
            S.emit(A_, lambda E, pp=pp, bank=bank: E.copy(out=h1r[pp][:], in_=ps[bank][:, 0:TC]))
            cp_ms = S.milestone(A_)
            bank_state[bank] = (A_, cp_ms)
            S.wait(V, S.sem[A_], cp_ms)
            S.emit(V, lambda E, ht=ht, pp=pp: E.tensor_copy(out=h1buf[pp][:, 0:2], in_=stash[:, ht * 2:ht * 2 + 2]))
            S.emit(V, lambda E, pp=pp: E.tensor_tensor(out=h1buf[pp][:, 2:2 + TC], in0=h1r[pp][:],
                                                       in1=s1_rep[:, 256:512], op=ALU.mult))
            S.vguard()
            S.emit(V, lambda E, ht=ht, pp=pp: E.tensor_copy(out=stash[:, ht * 2:ht * 2 + 2], in_=h1buf[pp][:, TC:TC + 2]))
            if debug and b == 0 and k == dbg_k and ht == 20:
                dbg_dump(25, h1buf[pp][:, 0:259], 259)
            c2o = 300 + (ht % 2) * 300
            S.emit(V, lambda E, ht=ht, pp=pp, L=L, ilo=ilo: E.scalar_tensor_tensor(
                out=scr3[:, 0:L], in0=h1buf[pp][:, ilo - 1:ilo - 1 + L], scalar=convc[:, ht:ht + 1],
                in1=h1buf[pp][:, ilo:ilo + L], op0=ALU.mult, op1=ALU.add))
            S.emit(V, lambda E, ht=ht, pp=pp, L=L, ilo=ilo, c2o=c2o: E.scalar_tensor_tensor(
                out=scr3[:, c2o:c2o + L], in0=h1buf[pp][:, ilo + 1:ilo + 1 + L], scalar=convc[:, NHT + ht:NHT + ht + 1],
                in1=scr3[:, 0:L], op0=ALU.mult, op1=ALU.add))
            if b == 0 and k == dbg_k and ht == 0:
                dbg_dump(15, h1buf[0][:, 0:259], 259)
                dbg_dump(16, scr3[:, c2o:c2o + L], L)
            S.need(A_, V)
            S.emit(A_, lambda E, ht=ht, L=L, c2o=c2o: E.activation(
                out=hs[ht][:, 0:L], in_=scr3[:, c2o:c2o + L], func=AF.Silu,
                scale=convc[:, 2 * NHT + ht:2 * NHT + ht + 1], bias=convc[:, 3 * NHT + ht:3 * NHT + ht + 1]))
            if b == 0 and k == dbg_k and ht == 0:
                dbg_dump(17, hs[0][:, 0:257], 257)
        return L

    def ffn_p2(b, k, g, L):
        slot = g % 3
        pslot = (g - 1) % 3
        uslot = k % 2
        S.need(V, A_)
        S.emit(V, lambda E, L=L: E.reduce_max(
            out=tmax[:, 0:L],
            in_=scanA[:, 0:15 * 258].rearrange("p (ht t) -> p ht t", t=258)[:, :, 0:L].transpose([0, 2, 1]),
            axis=AX.X, apply_absolute_value=True))
        S.emit(V, lambda E, L=L: E.reduce_max(
            out=scr3[:, 0:L],
            in_=scanB[:, 0:15 * 258].rearrange("p (ht t) -> p ht t", t=258)[:, :, 0:L].transpose([0, 2, 1]),
            axis=AX.X, apply_absolute_value=True))
        S.emit(V, lambda E, L=L: E.tensor_tensor(out=tmax[:, 0:L], in0=tmax[:, 0:L], in1=scr3[:, 0:L], op=ALU.max))
        S.emit(V, lambda E, L=L: E.reduce_max(
            out=scr3[:, 0:L],
            in_=hs_x[:, 0:516].rearrange("p (ht t) -> p ht t", t=258)[:, :, 0:L].transpose([0, 2, 1]),
            axis=AX.X, apply_absolute_value=True))
        S.emit(V, lambda E, L=L: E.tensor_tensor(out=tmax[:, 0:L], in0=tmax[:, 0:L], in1=scr3[:, 0:L], op=ALU.max))
        if b == 0 and k == dbg_k:
            dbg_dump(23, hs[20][:, 0:257], 257)
            dbg_dump(18, tmax[:, 0:384], 384)
        quant_cols(3, 1, srow, tmax, L)
        S.need(T_, V)
        bank_guard(T_, bank_state, PS_LN0)
        bank_guard(T_, bank_state, PS_LN1)
        S.emit(T_, lambda E: E.matmul(out=ps[PS_LN1][:, 0:384], lhsT=ones_rowf[:], rhs=srow[:, 0:384], start=True, stop=True))
        S.emit(T_, lambda E: E.matmul(out=ps[PS_LN0][:, 0:384], lhsT=ones_rowf[:], rhs=srow[:, 384:768], start=True, stop=True))
        S.need(V, T_)
        S.emit(V, lambda E: E.tensor_copy(out=s2_rep[:], in_=ps[PS_LN1][:, 0:384]))
        S.emit(V, lambda E: E.tensor_copy(out=si2_rep[:], in_=ps[PS_LN0][:, 0:384]))
        if b == 0 and k == dbg_k:
            dbg_dump(19, s2_rep[:], 384)
            dbg_dump(20, si2_rep[:], 384)
        bank_state[PS_LN0] = (V, S.milestone(V))
        bank_state[PS_LN1] = (V, S.cnt[V])
        for ht in range(NHT):
            S.emit(V, lambda E, ht=ht, L=L: E.tensor_tensor(out=scr3[:, 0:L], in0=hs[ht][:, 0:L], in1=s2_rep[:, 0:L], op=ALU.mult))
            S.emit(V, lambda E, ht=ht, L=L: E.tensor_scalar(out=hq[ht][:, 0:L], in0=scr3[:, 0:L],
                                                            scalar1=MAGIC, scalar2=MAGIC, op0=ALU.add, op1=ALU.subtract))
        cw2 = {}
        f2_ms = {}
        cw2[0] = S.dma(SY, w2c[0][:], w2ts[0].rearrange("p ht c -> p (ht c)"))
        mv_q = S.milestone(V)
        for dt in range(NDT):
            pp = dt % 2
            if dt + 1 < NDT:
                if dt >= 1:
                    S.wait(SY, S.sem[T_], S.cnt[T_])
                cw2[dt + 1] = S.dma(SY, w2c[(dt + 1) % 2][:],
                                    w2ts[dt + 1].rearrange("p ht c -> p (ht c)"))
            if debug and b == 0 and k == dbg_k and dt == 0:
                S.wait(G, S.sem[V], mv_q)
                S.dma(G, dbg_h[21, :, 0:257], hq[0])
                S.wait_dma(G, SY, cw2[0])
                S.dma(G, dbg_h[22, :, 0:4096], w2c[0][:])
            S.wait_dma(T_, SY, cw2[dt])
            if dt == 0:
                S.wait(T_, S.sem[V], mv_q)
            bank = PS_F2A if pp == 0 else PS_F2B
            bank_guard(T_, bank_state, bank)
            for ht in range(NHT):
                S.emit(T_, lambda E, ht=ht, pp=pp, bank=bank, L=L: E.matmul(
                    out=ps[bank][:, 0:L], lhsT=w2c[pp][:, ht * 128:(ht + 1) * 128],
                    rhs=hq[ht][:, 0:L], start=(ht == 0), stop=(ht == NHT - 1)))
            S.need(A_, T_)
            S.emit(A_, lambda E, dt=dt, bank=bank, L=L: E.copy(out=f2s[dt][:, 0:L], in_=ps[bank][:, 0:L]))
            f2_ms[dt] = S.milestone(A_)
            bank_state[bank] = (A_, f2_ms[dt])
        return f2_ms, L

    def ffn_tail(b, k, g, ctx2):
        slot = g % 3
        pslot = (g - 1) % 3
        f2_ms, L = ctx2
        for dt in range(NDT):
            S.wait(V, S.sem[A_], f2_ms[dt])
            S.emit(V, lambda E, dt=dt, L=L: E.tensor_tensor(out=scr3[:, 0:L], in0=f2s[dt][:, 0:L],
                                                            in1=si2_rep[:, 0:L], op=ALU.mult))
            if k == 0:
                S.emit(V, lambda E, dt=dt, slot=slot, L=L: E.tensor_tensor(out=xb[slot][dt][:, 0:L], in0=xb[slot][dt][:, 0:L],
                                                                in1=scr3[:, 0:L], op=ALU.add))
            else:
                S.emit(V, lambda E, dt=dt, pslot=pslot: E.tensor_tensor(
                    out=xb[pslot][dt][:, TC - 1:TC], in0=xb[pslot][dt][:, TC - 1:TC], in1=scr3[:, 0:1], op=ALU.add))
                S.emit(V, lambda E, dt=dt, slot=slot, L=L: E.tensor_tensor(
                    out=xb[slot][dt][:, 0:L - 1], in0=xb[slot][dt][:, 0:L - 1], in1=scr3[:, 1:L], op=ALU.add))
        mv = S.milestone(V)
        S.wait(G, S.sem[V], mv)
        cg = 0
        if k >= 1:
            for dt in range(NDT):
                cg = S.dma(G, xo_hbm[b, dt * 128:(dt + 1) * 128, (k - 1) * TC:k * TC], xb[pslot][dt][:])
            xout_counts[g - 1] = cg
        if k == NCH - 1:
            for dt in range(NDT):
                cg = S.dma(G, xo_hbm[b, dt * 128:(dt + 1) * 128, k * TC:(k + 1) * TC], xb[slot][dt][:])
            xout_counts[g] = cg

    chunks = [(b, k) for b in range(nb) for k in range(NCH)]
    pend_tail = None
    for g, (b, k) in enumerate(chunks):
        S.squeue_alias = "s" if g % 2 == 0 else "s1"
        if pend_tail is not None:
            ffn_tail(*pend_tail)
            pend_tail = None
        tms = front_a(b, k, g)
        ctx = mixer_front(b, k, g, tms)
        if g > 0:
            pb, pk = chunks[g - 1]
            Lp = ffn_p1(pb, pk, g - 1)
            ctx2 = ffn_p2(pb, pk, g - 1, Lp)
            pend_tail = (pb, pk, g - 1, ctx2)
        mixer_back(b, k, g, ctx)
    if pend_tail is not None:
        ffn_tail(*pend_tail)
    gl = len(chunks) - 1
    Lp = ffn_p1(chunks[-1][0], chunks[-1][1], gl)
    ctx2 = ffn_p2(chunks[-1][0], chunks[-1][1], gl, Lp)
    ffn_tail(chunks[-1][0], chunks[-1][1], gl, ctx2)
    S.wait(G, S.dsem["g"], S.dcnt["g"])
    S.wait(G, S.dsem["s"], S.dcnt["s"])
    S.wait(G, S.dsem["s1"], S.dcnt["s1"])
    S.wait(G, S.dsem["a"], S.dcnt["a"])

    with nc.Block() as block:
        S.run(block)
    stack.close()
    return nc


_NC_CACHE = {}


def kernel(L0_attn_x, L0_attn_A_log, L0_attn_D_skip, L0_attn_Wd, L0_attn_bd,
           L0_attn_Wb, L0_attn_bb, L0_attn_Wc, L0_attn_bc,
           L0_ffn_w1, L0_ffn_conv_w, L0_ffn_conv_b, L0_ffn_w2):
    x = np.asarray(L0_attn_x, dtype=np.float32)
    NB_CORES = 8
    T = x.shape[1]
    if "nc" not in _NC_CACHE:
        _NC_CACHE["nc"] = build_program(T=T)
    nc = _NC_CACHE["nc"]
    common = {
        "A_log": np.ascontiguousarray(L0_attn_A_log, dtype=np.float32),
        "Dskip": np.ascontiguousarray(L0_attn_D_skip, dtype=np.float32),
        "Wd": np.ascontiguousarray(L0_attn_Wd, dtype=np.float32),
        "bd": np.ascontiguousarray(np.asarray(L0_attn_bd, dtype=np.float32).reshape(1, D)),
        "Wb": np.ascontiguousarray(L0_attn_Wb, dtype=np.float32),
        "bb": np.ascontiguousarray(np.asarray(L0_attn_bb, dtype=np.float32).reshape(1, N)),
        "Wc": np.ascontiguousarray(L0_attn_Wc, dtype=np.float32),
        "bc": np.ascontiguousarray(np.asarray(L0_attn_bc, dtype=np.float32).reshape(1, N)),
        "w1": np.ascontiguousarray(L0_ffn_w1, dtype=np.float32),
        "cw": np.ascontiguousarray(np.asarray(L0_ffn_conv_w, dtype=np.float32).reshape(H, 3)),
        "cb": np.ascontiguousarray(np.asarray(L0_ffn_conv_b, dtype=np.float32).reshape(H)),
        "w2": np.ascontiguousarray(L0_ffn_w2, dtype=np.float32),
        "ident": np.eye(128, dtype=np.float32),
    }
    in_maps = []
    for c in range(NB_CORES):
        xc = np.ascontiguousarray(x[B_LOC * c:B_LOC * (c + 1)].transpose(0, 2, 1))
        m = dict(common)
        m["x"] = xc
        in_maps.append(m)
    res = run_bass_kernel_spmd(nc, in_maps, core_ids=list(range(NB_CORES)))
    outs = [np.asarray(res.results[c]["xo"]).transpose(0, 2, 1) for c in range(NB_CORES)]
    return np.ascontiguousarray(np.concatenate(outs, axis=0)).astype(np.float32)


# version 10 (final)
# speedup vs baseline: 1.0205x; 1.0205x over previous
import numpy as np
import concourse.bass as bass
import concourse.mybir as mybir
from concourse.bass_utils import run_bass_kernel_spmd
from contextlib import ExitStack

F32 = mybir.dt.float32
BF16 = mybir.dt.bfloat16
AF = mybir.ActivationFunctionType
ALU = mybir.AluOpType
AX = mybir.AxisListType

B_LOC = 2
D = 1024
N = 16
H = 4096
TC = 256
NDT = D // 128
NHT = H // 128
MAGIC = float(1.5 * 2 ** 23)

ENGS = ("v", "a", "t", "g", "s")


class Sched:

    def __init__(self, nc, stack):
        self.nc = nc
        self.q = {e: [] for e in ENGS}
        self.sem = {e: stack.enter_context(nc.semaphore(f"ms_{e}")) for e in ENGS}
        self.cnt = {e: 0 for e in ENGS}
        self.emitted = {e: 0 for e in ENGS}
        self.flushed = {e: 0 for e in ENGS}
        self.dsem = {e: stack.enter_context(nc.semaphore(f"ds_{e}")) for e in ("s", "s1", "a", "g")}
        self.dcnt = {e: 0 for e in ("s", "s1", "a", "g")}
        self.squeue_alias = "s"

    def emit(self, eng, fn):
        self.q[eng].append(["i", fn])
        self.emitted[eng] += 1

    def milestone(self, eng):
        if self.emitted[eng] == self.flushed[eng]:
            return self.cnt[eng]
        for i in range(len(self.q[eng]) - 1, -1, -1):
            if self.q[eng][i][0] == "i":
                fn = self.q[eng][i][1]
                sem = self.sem[eng]

                def wrapped(E, old=fn, sem=sem):
                    ins = old(E)
                    ins.then_inc(sem, 1)
                    return ins
                self.q[eng][i][1] = wrapped
                break
        self.cnt[eng] += 1
        self.flushed[eng] = self.emitted[eng]
        return self.cnt[eng]

    def vguard(self, eng="v"):
        c = self.milestone(eng)
        self.wait(eng, self.sem[eng], c)

    def wait(self, eng, sem, val):
        if val <= 0:
            return
        self.q[eng].append(["w", (sem, val)])

    def need(self, consumer, producer):
        if producer == consumer:
            return
        c = self.milestone(producer)
        self.wait(consumer, self.sem[producer], c)

    def fence_all(self, engines=("v", "a", "t")):
        cs = {e: self.milestone(e) for e in engines}
        for e in engines:
            for p in engines:
                if p != e:
                    self.wait(e, self.sem[p], cs[p])

    def dma(self, queue, out, in_, **kw):
        slot = self.squeue_alias if queue == "s" else queue
        sem = self.dsem[slot]

        def f(E):
            return E.dma_start(out=out, in_=in_, **kw).then_inc(sem, 16)
        self.q[queue].append(["i", f])
        self.emitted[queue] += 1
        self.dcnt[slot] += 16
        return (sem, self.dcnt[slot])

    def wait_dma(self, eng, queue_or_token, val=None):
        if isinstance(queue_or_token, tuple):
            sem, v = queue_or_token
        elif isinstance(val, tuple):
            sem, v = val
        else:
            slot = self.squeue_alias if queue_or_token == "s" else queue_or_token
            sem, v = self.dsem[slot], val
        self.wait(eng, sem, v)

    def run(self, block):
        def mk(eng):
            def body(E):
                for kind, payload in self.q[eng]:
                    if kind == "i":
                        payload(E)
                    else:
                        sem, val = payload
                        E.wait_ge(sem, val)
            return body
        block.vector(mk("v"))
        block.scalar(mk("a"))
        block.tensor(mk("t"))
        block.gpsimd(mk("g"))
        block.sync(mk("s"))


def build_program(T=2048, debug=False, dbg_k=0):
    NCH = T // TC
    nb = B_LOC
    nc = bass.Bass("TRN2", target_bir_lowering=False)

    x_hbm = nc.dram_tensor("x", [nb, D, T], F32, kind="ExternalInput").ap()
    xo_hbm = nc.dram_tensor("xo", [nb, D, T], F32, kind="ExternalOutput").ap()
    A_log = nc.dram_tensor("A_log", [D, N], F32, kind="ExternalInput").ap()
    Dskip_h = nc.dram_tensor("Dskip", [D], F32, kind="ExternalInput").ap()
    Wd_h = nc.dram_tensor("Wd", [D, D], F32, kind="ExternalInput").ap()
    bd_h = nc.dram_tensor("bd", [1, D], F32, kind="ExternalInput").ap()
    Wb_h = nc.dram_tensor("Wb", [D, N], F32, kind="ExternalInput").ap()
    bb_h = nc.dram_tensor("bb", [1, N], F32, kind="ExternalInput").ap()
    Wc_h = nc.dram_tensor("Wc", [D, N], F32, kind="ExternalInput").ap()
    bc_h = nc.dram_tensor("bc", [1, N], F32, kind="ExternalInput").ap()
    w1_h = nc.dram_tensor("w1", [H, D], F32, kind="ExternalInput").ap()
    cw_h = nc.dram_tensor("cw", [H, 3], F32, kind="ExternalInput").ap()
    cb_h = nc.dram_tensor("cb", [H], F32, kind="ExternalInput").ap()
    w2_h = nc.dram_tensor("w2", [D, H], F32, kind="ExternalInput").ap()
    ident_h = nc.dram_tensor("ident", [128, 128], F32, kind="ExternalInput").ap()

    dbg_h = nc.dram_tensor("dbg", [26, 128, 4096], F32, kind="ExternalOutput").ap() if debug else None
    bc_scr = nc.dram_tensor("bc_scr", [2, N, TC], F32, kind="Internal").ap()
    b_scr = nc.dram_tensor("b_scr", [N, TC], BF16, kind="Internal").ap()
    c_scr = nc.dram_tensor("c_scr", [N, TC], BF16, kind="Internal").ap()
    w1slab = nc.dram_tensor("w1slab", [NDT, H, 128], BF16, kind="Internal").ap()
    w1ts = nc.dram_tensor("w1ts", [NHT, 128, NDT, 128], BF16, kind="Internal").ap()
    w2q_d = nc.dram_tensor("w2q_d", [D, H], BF16, kind="Internal").ap()
    w2ts = nc.dram_tensor("w2ts", [NDT, 128, NHT, 128], BF16, kind="Internal").ap()

    stack = ExitStack()
    e = stack.enter_context
    S = Sched(nc, stack)

    xb = [[e(nc.sbuf_tensor(f"xb{s}_{d}", [128, TC], F32)) for d in range(NDT)] for s in range(3)]
    ubig = e(nc.sbuf_tensor("ubig", [128, NDT * TC], F32))
    ub = [ubig[:, d * TC:(d + 1) * TC] for d in range(NDT)]
    ubf = [[e(nc.sbuf_tensor(f"ubf{u}_{d}", [128, TC], BF16)) for d in range(NDT)] for u in range(2)]
    delta = [e(nc.sbuf_tensor(f"dl{d}", [128, TC], F32)) for d in range(NDT)]
    scanA = e(nc.sbuf_tensor("scanA", [128, N * TC], F32))
    scanB = e(nc.sbuf_tensor("scanB", [128, N * TC], F32))
    scanH = e(nc.sbuf_tensor("scanH", [128, N * TC], BF16))
    Brep = e(nc.sbuf_tensor("Brep", [128, N * TC], F32))
    Crep = e(nc.sbuf_tensor("Crep", [128, N * TC], BF16))
    hs_x = e(nc.sbuf_tensor("hs_x", [128, 516], F32))
    hqd = e(nc.sbuf_tensor("hqd", [128, NHT * 258], BF16, is_transpose_dest=True))
    hcarry = [e(nc.sbuf_tensor(f"hc{d}", [128, N], F32)) for d in range(NDT)]
    injb = e(nc.sbuf_tensor("injb", [128, N], F32))
    scr1 = e(nc.sbuf_tensor("scr1", [128, TC], F32))
    scr2 = e(nc.sbuf_tensor("scr2", [128, TC], F32))
    scr2b = e(nc.sbuf_tensor("scr2b", [128, TC], F32))
    scr3 = e(nc.sbuf_tensor("scr3", [128, 900], F32))
    Bsb = e(nc.sbuf_tensor("Bsb", [N, TC], BF16))
    Csb = e(nc.sbuf_tensor("Csb", [N, TC], BF16))
    rows = e(nc.sbuf_tensor("rows", [1, 1024], F32))
    mr_rep = e(nc.sbuf_tensor("mr_rep", [128, 512], F32))
    s1_rep = e(nc.sbuf_tensor("s1_rep", [128, 512], F32))
    s2_rep = e(nc.sbuf_tensor("s2_rep", [128, 384], F32))
    si2_rep = e(nc.sbuf_tensor("si2_rep", [128, 384], F32))
    tmax = e(nc.sbuf_tensor("tmax", [128, 384], F32))
    scol = e(nc.sbuf_tensor("scol", [128, 16], F32))
    srow = e(nc.sbuf_tensor("srow", [1, 768], F32))
    hs = []
    for i in range(15):
        hs.append(scanA[:, i * 258:i * 258 + 257])
    for i in range(15):
        hs.append(scanB[:, i * 258:i * 258 + 257])
    hs.append(hs_x[:, 0:257])
    hs.append(hs_x[:, 258:515])
    hq = [hqd[:, i * 258:i * 258 + 257] for i in range(NHT)]
    hC_bf = scanA[:].bitcast(BF16)
    h1buf = [e(nc.sbuf_tensor(f"h1b{i}", [128, 259], F32)) for i in range(2)]
    h1r = [e(nc.sbuf_tensor(f"h1r{i}", [128, TC], F32)) for i in range(2)]
    f2s = [e(nc.sbuf_tensor(f"f2s{i}", [128, 257], F32)) for i in range(NDT)]
    stash = e(nc.sbuf_tensor("stash", [128, 2 * NHT], F32))
    w1c = [e(nc.sbuf_tensor(f"w1c{i}", [128, 8 * 128], BF16)) for i in range(3)]
    w2c = [e(nc.sbuf_tensor(f"w2c{i}", [128, H], BF16)) for i in range(2)]
    Wd_bf = [e(nc.sbuf_tensor(f"Wdb{d}", [128, D], BF16)) for d in range(NDT)]
    Wb_bf = [e(nc.sbuf_tensor(f"Wbb{d}", [128, N], BF16)) for d in range(NDT)]
    Wc_bf = [e(nc.sbuf_tensor(f"Wcb{d}", [128, N], BF16)) for d in range(NDT)]
    A_sb = e(nc.sbuf_tensor("A_sb", [128, NDT * N], F32))
    Dsk_sb = e(nc.sbuf_tensor("Dsk_sb", [128, NDT], F32))
    convc = e(nc.sbuf_tensor("convc", [128, 4 * NHT], F32))
    bdc = e(nc.sbuf_tensor("bdc", [128, NDT], F32))
    bbc_c = e(nc.sbuf_tensor("bbc_c", [N, 2], F32))
    ident = e(nc.sbuf_tensor("ident_sb", [128, 128], F32))
    ones_col = e(nc.sbuf_tensor("ones_col", [128, 1], F32))
    ones_rowf = e(nc.sbuf_tensor("ones_rowf", [1, 128], F32))
    sw_sc = e(nc.sbuf_tensor("sw_sc", [1, 8], F32))
    swcol = e(nc.sbuf_tensor("swcol", [128, 2], F32))
    smallw = e(nc.sbuf_tensor("smallw", [128, 16], F32))
    cst = e(nc.sbuf_tensor("cst", [1, 4], F32))
    wq_buf = hqd[:, 0:H]

    ps = [e(nc.psum_tensor(f"ps{i}", [128, 512], F32)) for i in range(8)]
    PS_LN0, PS_LN1, PS_PRJ0, PS_PRJ1, PS_F1A, PS_F1B, PS_F2A, PS_F2B = range(8)

    V, A_, T_, G, SY = "v", "a", "t", "g", "s"

    dbg_used = [0]

    def dbg_dump(idx, ap, cols, rows=128):
        if not debug:
            return
        S.need(G, V)
        S.need(G, A_)
        S.need(G, T_)
        c = S.dma(G, dbg_h[idx, 0:rows, 0:cols], ap)
        S.wait_dma(V, G, c)
        S.wait_dma(A_, G, c)
        S.wait_dma(T_, G, c)
        dbg_used[0] = max(dbg_used[0], idx + 1)

    S.emit(V, lambda E: E.memset(ones_col[:], 1.0))
    S.emit(V, lambda E: E.memset(ones_rowf[:], 1.0))
    S.emit(V, lambda E: E.memset(cst[:, 0:1], 1e-5))
    S.emit(V, lambda E: E.memset(cst[:, 1:2], -0.5))
    S.emit(V, lambda E: E.memset(h1buf[0][:, 258:259], 0.0))
    S.emit(V, lambda E: E.memset(h1buf[1][:, 258:259], 0.0))
    S.dma(SY, ident[:], ident_h)

    for d in range(NDT):
        c = S.dma(SY, scanA[:, 0:D], Wd_h[d * 128:(d + 1) * 128, :])
        S.wait_dma(V, SY, c)
        S.emit(V, lambda E, d=d: E.tensor_copy(out=Wd_bf[d][:], in_=scanA[:, 0:D]))
        S.wait(SY, S.sem[V], S.milestone(V))
    for d in range(NDT):
        c = S.dma(SY, scanA[:, 0:N], Wb_h[d * 128:(d + 1) * 128, :])
        c = S.dma(SY, scanA[:, N:2 * N], Wc_h[d * 128:(d + 1) * 128, :])
        c = S.dma(SY, scanA[:, 2 * N:3 * N], A_log[d * 128:(d + 1) * 128, :])
        S.wait_dma(V, SY, c)
        S.emit(V, lambda E, d=d: E.tensor_copy(out=Wb_bf[d][:], in_=scanA[:, 0:N]))
        S.emit(V, lambda E, d=d: E.tensor_copy(out=Wc_bf[d][:], in_=scanA[:, N:2 * N]))
        S.need(A_, V)
        S.emit(A_, lambda E: E.activation(out=scanA[:, 3 * N:4 * N], in_=scanA[:, 2 * N:3 * N], func=AF.Exp))
        S.need(V, A_)
        S.emit(V, lambda E, d=d: E.tensor_scalar(out=A_sb[:, d * N:(d + 1) * N], in0=scanA[:, 3 * N:4 * N],
                                                 scalar1=-1.0, scalar2=None, op0=ALU.mult))
        S.vguard()
        S.wait(SY, S.sem[V], S.milestone(V))
    for d in range(NDT):
        S.dma(SY, Dsk_sb[:, d:d + 1], Dskip_h[d * 128:(d + 1) * 128].rearrange("(p a) -> p a", a=1),
              allow_slow_non_contiguous=True)
    for et in range(NDT):
        S.dma(SY, bdc[:, et:et + 1], bd_h[0:1, et * 128:(et + 1) * 128].rearrange("a p -> p a"),
              allow_slow_non_contiguous=True)
    S.dma(SY, bbc_c[:, 0:1], bb_h.rearrange("a p -> p a"), allow_slow_non_contiguous=True)
    S.dma(SY, bbc_c[:, 1:2], bc_h.rearrange("a p -> p a"), allow_slow_non_contiguous=True)
    for i in range(NHT):
        c = S.dma(SY, smallw[:, 0:3], cw_h[i * 128:(i + 1) * 128, :])
        c = S.dma(SY, convc[:, 3 * NHT + i:3 * NHT + i + 1],
                  cb_h[i * 128:(i + 1) * 128].rearrange("(p a) -> p a", a=1),
                  allow_slow_non_contiguous=True)
        S.wait_dma(V, SY, c)
        S.emit(V, lambda E: E.tensor_scalar(out=smallw[:, 4:5], in0=smallw[:, 1:2], scalar1=-1.0, scalar2=None, op0=ALU.mult))
        S.vguard()
        S.emit(V, lambda E: E.tensor_tensor(out=smallw[:, 4:5], in0=smallw[:, 4:5], in1=smallw[:, 1:2], op=ALU.max))
        S.vguard()
        S.emit(V, lambda E: E.tensor_scalar(out=smallw[:, 5:6], in0=smallw[:, 4:5], scalar1=1e-20, scalar2=None, op0=ALU.is_lt))
        S.vguard()
        S.emit(V, lambda E: E.tensor_scalar(out=smallw[:, 6:7], in0=smallw[:, 5:6], scalar1=-1.0, scalar2=1.0, op0=ALU.mult, op1=ALU.add))
        S.vguard()
        S.emit(V, lambda E: E.tensor_tensor(out=smallw[:, 7:8], in0=smallw[:, 1:2], in1=smallw[:, 6:7], op=ALU.mult))
        S.vguard()
        S.emit(V, lambda E: E.tensor_scalar(out=smallw[:, 8:9], in0=smallw[:, 5:6], scalar1=1e-20, scalar2=None, op0=ALU.mult))
        S.vguard()
        S.emit(V, lambda E, i=i: E.tensor_tensor(out=convc[:, 2 * NHT + i:2 * NHT + i + 1], in0=smallw[:, 7:8], in1=smallw[:, 8:9], op=ALU.add))
        S.vguard()
        S.emit(V, lambda E, i=i: E.reciprocal(out=smallw[:, 9:10], in_=convc[:, 2 * NHT + i:2 * NHT + i + 1]))
        S.vguard()
        S.emit(V, lambda E, i=i: E.tensor_tensor(out=convc[:, i:i + 1], in0=smallw[:, 0:1], in1=smallw[:, 9:10], op=ALU.mult))
        S.vguard()
        S.emit(V, lambda E, i=i: E.tensor_tensor(out=convc[:, NHT + i:NHT + i + 1], in0=smallw[:, 2:3], in1=smallw[:, 9:10], op=ALU.mult))
        S.vguard()
        S.wait(SY, S.sem[V], S.milestone(V))

    def sy_sync_compute():
        for eng in (V, A_, T_):
            c = S.milestone(eng)
            S.wait(SY, S.sem[eng], c)

    def wsum(w_src, n_tiles, tile_cols, mean_div, out_slot):
        nmm = n_tiles * (tile_cols // 512)
        k = [0]
        stg = [scanA, Brep]
        toks = {}
        a_ms = {}
        t_ms = {}
        sy_sync_compute()
        for i in range(min(2, n_tiles)):
            toks[i] = S.dma(SY, stg[i % 2][:, 0:tile_cols], w_src[i * 128:(i + 1) * 128, :])
        for i in range(n_tiles):
            S.wait_dma(A_, SY, toks[i])
            S.emit(A_, lambda E, i=i, tc=tile_cols: E.activation(out=scanB[:, 0:tc], in_=stg[i % 2][:, 0:tc], func=AF.Abs))
            a_ms[i] = S.milestone(A_)
            if i + 2 < n_tiles:
                S.wait(SY, S.sem[A_], a_ms[i])
                toks[i + 2] = S.dma(SY, stg[i % 2][:, 0:tile_cols], w_src[(i + 2) * 128:(i + 3) * 128, :])
            S.wait(T_, S.sem[A_], a_ms[i])
            for j in range(0, tile_cols, 512):
                st = (k[0] == 0)
                sp = (k[0] == nmm - 1)
                k[0] += 1
                S.emit(T_, lambda E, j=j, st=st, sp=sp: E.matmul(out=ps[PS_LN0][0:1, 0:512], lhsT=ones_col[:],
                                                                 rhs=scanB[:, j:j + 512], start=st, stop=sp))
            t_ms[i] = S.milestone(T_)
            S.wait(A_, S.sem[T_], t_ms[i])
        S.need(V, T_)
        S.emit(V, lambda E: E.tensor_copy(out=rows[:, 0:512], in_=ps[PS_LN0][0:1, 0:512]))
        S.vguard()
        S.emit(V, lambda E: E.reduce_sum(out=rows[:, 512:513], in_=rows[:, 0:512], axis=AX.X))
        S.vguard()
        S.emit(V, lambda E, md=mean_div: E.tensor_scalar(out=rows[:, 513:514], in0=rows[:, 512:513],
                                                         scalar1=1.0 / md, scalar2=1e-5, op0=ALU.mult, op1=ALU.max))
        S.vguard()
        S.emit(V, lambda E, o=out_slot: E.tensor_copy(out=sw_sc[:, o + 1:o + 2], in_=rows[:, 513:514]))
        S.vguard()
        S.emit(V, lambda E, o=out_slot: E.reciprocal(out=sw_sc[:, o:o + 1], in_=rows[:, 513:514]))
        S.vguard()

    wsum(w1_h, NHT, D, H * D, 0)
    wsum(w2_h, NDT, H, H * D, 2)
    S.need(T_, V)
    S.emit(T_, lambda E: E.matmul(out=ps[PS_LN1][:, 0:8], lhsT=ones_rowf[:], rhs=sw_sc[:, 0:8], start=True, stop=True))
    S.need(V, T_)
    S.emit(V, lambda E: E.tensor_copy(out=swcol[:, 0:1], in_=ps[PS_LN1][:, 1:2]))
    S.vguard()
    S.emit(V, lambda E: E.tensor_copy(out=swcol[:, 1:2], in_=ps[PS_LN1][:, 3:4]))
    S.vguard()
    S.emit(V, lambda E: E.tensor_copy(out=smallw[:, 10:11], in_=ps[PS_LN1][:, 0:1]))
    S.vguard()
    S.emit(V, lambda E: E.tensor_copy(out=smallw[:, 11:12], in_=ps[PS_LN1][:, 2:3]))
    S.vguard()

    qv = wq_buf
    ldt = {}
    stt_ = {}
    vms = {}
    sy_sync_compute()
    for i in range(4):
        ldt[i] = S.dma(SY, scanA[:, (i % 4) * D:(i % 4 + 1) * D], w1_h[i * 128:(i + 1) * 128, :])
    for i in range(NHT):
        j = i % 4
        S.wait_dma(V, SY, ldt[i])
        if i >= 4 and (i - 4) in stt_:
            S.wait_dma(V, stt_[i - 4])
        S.emit(V, lambda E, j=j: E.tensor_scalar(out=scanB[:, j * D:(j + 1) * D], in0=scanA[:, j * D:(j + 1) * D],
                                            scalar1=smallw[:, 10:11], scalar2=MAGIC, op0=ALU.mult, op1=ALU.add))
        S.emit(V, lambda E, j=j: E.tensor_scalar(out=scanB[:, j * D:(j + 1) * D], in0=scanB[:, j * D:(j + 1) * D],
                                            scalar1=MAGIC, scalar2=1.0, op0=ALU.subtract, op1=ALU.min))
        S.emit(V, lambda E, j=j: E.tensor_scalar(out=qv[:, j * D:(j + 1) * D], in0=scanB[:, j * D:(j + 1) * D],
                                            scalar1=-1.0, scalar2=None, op0=ALU.max))
        vms[i] = S.milestone(V)
        S.wait(SY, S.sem[V], vms[i])
        stt_[i] = S.dma(SY, w1slab[:, i * 128:(i + 1) * 128, :].rearrange("ds p c -> p ds c"),
                        qv[:, j * D:(j + 1) * D])
        if i + 4 < NHT:
            ldt[i + 4] = S.dma(SY, scanA[:, j * D:(j + 1) * D], w1_h[(i + 4) * 128:(i + 5) * 128, :])
    cpre = S.dcnt["s"]
    S.wait_dma(SY, SY, cpre)
    for ds in range(NDT):
        c = S.dma(SY, qv[:, 0:H], w1slab[ds], transpose=True)
        S.wait_dma(SY, SY, c)
        c = S.dma(SY, w1ts[:, :, ds, :].rearrange("ht p c -> p ht c"),
                  qv[:, 0:H])
        S.wait_dma(SY, SY, c)
    stg2 = [scanA, Brep]
    out2 = [qv, hqd[:, H:2 * H]]
    ld2 = {}
    st2 = {}
    vm2 = {}
    sy_sync_compute()
    for i in range(min(2, NDT)):
        ld2[i] = S.dma(SY, stg2[i % 2][:, 0:H], w2_h[i * 128:(i + 1) * 128, :])
    for i in range(NDT):
        j = i % 2
        S.wait_dma(V, SY, ld2[i])
        if (i - 2) in st2:
            S.wait_dma(V, st2[i - 2])
        S.emit(V, lambda E, j=j: E.tensor_scalar(out=scanB[:, 0:H], in0=stg2[j][:, 0:H],
                                            scalar1=smallw[:, 11:12], scalar2=MAGIC, op0=ALU.mult, op1=ALU.add))
        S.emit(V, lambda E: E.tensor_scalar(out=scanB[:, 0:H], in0=scanB[:, 0:H],
                                            scalar1=MAGIC, scalar2=1.0, op0=ALU.subtract, op1=ALU.min))
        S.emit(V, lambda E, j=j: E.tensor_scalar(out=out2[j][:, 0:H] if j == 0 else out2[j], in0=scanB[:, 0:H],
                                            scalar1=-1.0, scalar2=None, op0=ALU.max))
        vm2[i] = S.milestone(V)
        S.wait(SY, S.sem[V], vm2[i])
        st2[i] = S.dma(SY, w2q_d[i * 128:(i + 1) * 128, :], out2[j][:, 0:H] if j == 0 else out2[j])
        if i + 2 < NDT:
            ld2[i + 2] = S.dma(SY, stg2[j][:, 0:H], w2_h[(i + 2) * 128:(i + 3) * 128, :])
    S.wait_dma(SY, st2[NDT - 1])
    for ht in range(NHT):
        c = S.dma(SY, qv[:, 0:D], w2q_d[:, ht * 128:(ht + 1) * 128], transpose=True)
        S.wait_dma(SY, SY, c)
        c = S.dma(SY, w2ts[:, :, ht, :].rearrange("ds p c -> p ds c"),
                  qv[:, 0:D])
        S.wait_dma(SY, SY, c)

    for eng in (V, A_, T_):
        S.wait_dma(eng, SY, S.dcnt["s"])
    S.fence_all()

    xout_counts = {}
    vq_ms = {}
    brep_free_v = [0]

    def bank_guard(engine_from, bank_state, bank):
        eng, cn = bank_state.get(bank, (None, 0))
        if eng is not None:
            S.wait(engine_from, S.sem[eng], cn)

    bank_state = {}

    def ln_sums(src_tiles, bank_a, bank_b):
        S.need(A_, T_)
        bank_guard(T_, bank_state, bank_a)
        bank_guard(T_, bank_state, bank_b)
        for dt in range(NDT):
            S.emit(A_, lambda E, dt=dt: E.activation(out=scr2b[:], in_=src_tiles[dt][:], func=AF.Square))
            S.need(T_, A_)
            S.emit(T_, lambda E, dt=dt, bank_a=bank_a: E.matmul(out=ps[bank_a][0:1, 0:TC], lhsT=ones_col[:],
                                                 rhs=src_tiles[dt][:], start=(dt == 0), stop=(dt == NDT - 1)))
            S.emit(T_, lambda E, dt=dt, bank_b=bank_b: E.matmul(out=ps[bank_b][0:1, 0:TC], lhsT=ones_col[:],
                                                 rhs=scr2b[:], start=(dt == 0), stop=(dt == NDT - 1)))
            S.need(A_, T_)
        tms = S.milestone(T_)
        return tms

    def ln_rest(src_tiles, dst_tiles, also_bf, bank_a, bank_b, tms):
        S.wait(V, S.sem[T_], tms)
        S.emit(V, lambda E, bank_a=bank_a: E.tensor_copy(out=rows[:, 0:TC], in_=ps[bank_a][0:1, 0:TC]))
        S.vguard()
        S.emit(V, lambda E, bank_b=bank_b: E.tensor_copy(out=rows[:, 256:256 + TC], in_=ps[bank_b][0:1, 0:TC]))
        S.vguard()
        bank_state[bank_a] = (V, S.milestone(V))
        bank_state[bank_b] = (V, S.cnt[V])
        S.emit(V, lambda E: E.tensor_scalar(out=rows[:, 512:768], in0=rows[:, 0:256], scalar1=1.0 / D, scalar2=None, op0=ALU.mult))
        S.vguard()
        S.emit(V, lambda E: E.tensor_scalar(out=rows[:, 256:512], in0=rows[:, 256:512], scalar1=1.0 / D, scalar2=None, op0=ALU.mult))
        S.vguard()
        S.emit(V, lambda E: E.tensor_tensor(out=rows[:, 0:256], in0=rows[:, 512:768], in1=rows[:, 512:768], op=ALU.mult))
        S.vguard()
        S.emit(V, lambda E: E.tensor_tensor(out=rows[:, 0:256], in0=rows[:, 256:512], in1=rows[:, 0:256], op=ALU.subtract))
        S.vguard()
        S.need(A_, V)
        S.emit(A_, lambda E: E.activation(out=rows[:, 256:512], in_=rows[:, 0:256], func=AF.Ln, bias=cst[:, 0:1]))
        S.vguard("a")
        S.emit(A_, lambda E: E.activation(out=rows[:, 768:1024], in_=rows[:, 256:512], func=AF.Exp, scale=cst[:, 1:2]))
        S.need(T_, A_)
        bank_guard(T_, bank_state, bank_a)
        S.emit(T_, lambda E, bank_a=bank_a: E.matmul(out=ps[bank_a][:, 0:512], lhsT=ones_rowf[:], rhs=rows[:, 512:1024], start=True, stop=True))
        S.need(V, T_)
        S.emit(V, lambda E, bank_a=bank_a: E.tensor_copy(out=mr_rep[:], in_=ps[bank_a][:, 0:512]))
        S.vguard()
        bank_state[bank_a] = (V, S.milestone(V))
        for dt in range(NDT):
            S.emit(V, lambda E, dt=dt: E.tensor_tensor(out=dst_tiles[dt], in0=src_tiles[dt][:], in1=mr_rep[:, 0:256], op=ALU.subtract))
            S.emit(V, lambda E, dt=dt: E.tensor_tensor(out=dst_tiles[dt], in0=dst_tiles[dt], in1=mr_rep[:, 256:512], op=ALU.mult))
            S.vguard()
            if also_bf is not None:
                S.emit(V, lambda E, dt=dt: E.tensor_copy(out=also_bf[dt][:], in_=dst_tiles[dt]))

    def quant_cols(n_chunks, sw_col_idx, srow_out, src_tiles_or_tmax, L, dbg_on=False):
        S.need(T_, V)
        bank_guard(T_, bank_state, PS_PRJ0)
        bank_guard(T_, bank_state, PS_PRJ1)
        for j in range(n_chunks):
            S.emit(T_, lambda E, j=j: E.transpose(out=ps[PS_PRJ0][:, j * 128:(j + 1) * 128],
                                                  in_=tmax[:, j * 128:(j + 1) * 128], identity=ident[:]))
        S.need(V, T_)
        for j in range(n_chunks):
            S.emit(V, lambda E, j=j: E.reduce_max(out=scol[:, j:j + 1], in_=ps[PS_PRJ0][:, j * 128:(j + 1) * 128],
                                                  axis=AX.X, apply_absolute_value=True))
            S.vguard()
        bank_state[PS_PRJ0] = (V, S.milestone(V))
        nch = n_chunks
        S.emit(V, lambda E: E.tensor_scalar(out=scol[:, 4:4 + nch], in0=scol[:, 0:nch], scalar1=1e-5, scalar2=None, op0=ALU.max))
        S.vguard()
        S.emit(V, lambda E: E.reciprocal(out=scol[:, 8:8 + nch], in_=scol[:, 4:4 + nch]))
        S.vguard()
        S.emit(V, lambda E: E.tensor_scalar(out=scol[:, 8:8 + nch], in0=scol[:, 8:8 + nch], scalar1=127.0, scalar2=None, op0=ALU.mult))
        S.vguard()
        S.emit(V, lambda E, si=sw_col_idx: E.tensor_scalar(out=scol[:, 12:12 + nch], in0=scol[:, 4:4 + nch],
                                                           scalar1=1.0 / 127.0, scalar2=swcol[:, si:si + 1],
                                                           op0=ALU.mult, op1=ALU.mult))
        S.vguard()
        S.need(T_, V)
        bank_guard(T_, bank_state, PS_PRJ0)
        for j in range(nch):
            S.emit(T_, lambda E, j=j: E.transpose(out=ps[PS_PRJ0][0:1, j * 128:(j + 1) * 128],
                                                  in_=scol[:, 8 + j:9 + j], identity=ident[:]))
        for j in range(nch):
            S.emit(T_, lambda E, j=j, o=384: E.transpose(out=ps[PS_PRJ1][0:1, j * 128:(j + 1) * 128],
                                                         in_=scol[:, 12 + j:13 + j], identity=ident[:]))
        S.need(V, T_)
        S.emit(V, lambda E, n=nch: E.tensor_copy(out=srow[:, 0:n * 128], in_=ps[PS_PRJ0][0:1, 0:n * 128]))
        S.vguard()
        S.emit(V, lambda E, n=nch: E.tensor_copy(out=srow[:, 384:384 + n * 128], in_=ps[PS_PRJ1][0:1, 0:n * 128]))
        S.vguard()
        bank_state[PS_PRJ0] = (V, S.milestone(V))
        bank_state[PS_PRJ1] = (V, S.cnt[V])
        if dbg_on:
            dbg_dump(8, scol[:], 16)
            dbg_dump(13, srow[:], 768, rows=1)

    def front_a(b, k, g):
        slot = g % 3
        if k == 0:
            for dt in range(NDT):
                S.emit(V, lambda E, dt=dt: E.memset(hcarry[dt][:], 0.0))
            S.vguard()
        if (g - 3) in xout_counts:
            S.wait_dma(SY, xout_counts[g - 3])
        cx = 0
        for dt in range(NDT):
            cx = S.dma(SY, xb[slot][dt][:], x_hbm[b, dt * 128:(dt + 1) * 128, k * TC:(k + 1) * TC])
        S.wait_dma(V, SY, cx)
        S.wait_dma(A_, SY, cx)
        S.wait_dma(T_, SY, cx)
        tms = ln_sums(xb[slot], PS_F2A, PS_F2B)
        return tms

    def mixer_front(b, k, g, tms):
        slot = g % 3
        uslot = k % 2
        delta_ms = {}
        ln_rest(xb[slot], ub, ubf[uslot], PS_F2A, PS_F2B, tms)
        if b == 0 and k == dbg_k:
            dbg_dump(0, ubig[:, 0:2048], 2048)
            dbg_dump(11, mr_rep[:], 512)
        S.need(T_, V)
        bank_guard(T_, bank_state, PS_PRJ1)
        for dk in range(NDT):
            S.emit(T_, lambda E, dk=dk, uslot=uslot: E.matmul(out=ps[PS_PRJ1][0:N, 0:TC], lhsT=Wb_bf[dk][:],
                                                 rhs=ubf[uslot][dk][:], start=(dk == 0), stop=(dk == NDT - 1)))
        for dk in range(NDT):
            S.emit(T_, lambda E, dk=dk, uslot=uslot: E.matmul(out=ps[PS_PRJ1][0:N, 256:256 + TC], lhsT=Wc_bf[dk][:],
                                                 rhs=ubf[uslot][dk][:], start=(dk == 0), stop=(dk == NDT - 1)))
        S.need(V, T_)
        S.emit(V, lambda E: E.tensor_scalar(out=Bsb[:], in0=ps[PS_PRJ1][0:N, 0:TC],
                                            scalar1=bbc_c[:, 0:1], scalar2=None, op0=ALU.add))
        S.emit(V, lambda E: E.tensor_scalar(out=Csb[:], in0=ps[PS_PRJ1][0:N, 256:256 + TC],
                                            scalar1=bbc_c[:, 1:2], scalar2=None, op0=ALU.add))
        S.vguard()
        bank_state[PS_PRJ1] = (V, S.milestone(V))
        S.wait("a", S.sem[V], S.cnt[V])
        cbo = S.dma("a", b_scr, Bsb[:])
        cbo = S.dma("a", c_scr, Csb[:])
        S.wait_dma("a", "a", cbo)
        S.wait(A_, S.sem[V], brep_free_v[0])
        cbi = S.dma("a", Brep[:].bitcast(BF16)[:, 0:N * TC], b_scr.rearrange("n t -> (n t)").unsqueeze(0).partition_broadcast(128))
        cbi = S.dma("a", Crep[:], c_scr.rearrange("n t -> (n t)").unsqueeze(0).partition_broadcast(128))
        for et in range(NDT):
            bank = PS_PRJ0 if et % 2 == 0 else PS_PRJ1
            bank_guard(T_, bank_state, bank)
            for dk in range(NDT):
                S.emit(T_, lambda E, dk=dk, et=et, bank=bank, uslot=uslot: E.matmul(
                    out=ps[bank][:, 0:TC], lhsT=Wd_bf[dk][:, et * 128:(et + 1) * 128],
                    rhs=ubf[uslot][dk][:], start=(dk == 0), stop=(dk == NDT - 1)))
            S.need(A_, T_)
            S.emit(A_, lambda E, bank=bank, et=et: E.activation(out=scr2[:], in_=ps[bank][:, 0:TC],
                                                               func=AF.Exp, bias=bdc[:, et:et + 1]))
            S.emit(A_, lambda E, et=et: E.activation(out=delta[et][:], in_=scr2[:], func=AF.Ln, bias=ones_col[:]))
            delta_ms[et] = S.milestone(A_)
            bank_state[bank] = (A_, delta_ms[et])
        return cbi, delta_ms

    def mixer_back(b, k, g, ctx):
        slot = g % 3
        uslot = k % 2
        cbi, delta_ms = ctx
        S.wait_dma(V, "a", cbi)
        if b == 0 and k == dbg_k:
            dbg_dump(1, delta[0][:], 256)
            dbg_dump(2, delta[7][:], 256)
            dbg_dump(9, Bsb[:], 256, rows=16)
            dbg_dump(10, Brep[:], 4096)
        for dt in range(NDT):
            S.wait(V, S.sem[A_], delta_ms[dt])
            S.emit(V, lambda E, dt=dt: E.tensor_tensor(out=scr1[:].bitcast(BF16)[:, 0:TC], in0=delta[dt][:], in1=ub[dt], op=ALU.mult))
            for n in range(8):
                S.emit(V, lambda E, dt=dt, n=n: E.tensor_scalar(
                    out=scanA[:, n * TC:(n + 1) * TC], in0=delta[dt][:],
                    scalar1=A_sb[:, dt * N + n:dt * N + n + 1], scalar2=None, op0=ALU.mult))
            S.need(A_, V)
            S.emit(A_, lambda E: E.activation(out=scanA[:, 0:8 * TC], in_=scanA[:, 0:8 * TC], func=AF.Exp))
            exp_ms1 = S.milestone(A_)
            for n in range(8, N):
                S.emit(V, lambda E, dt=dt, n=n: E.tensor_scalar(
                    out=scanA[:, n * TC:(n + 1) * TC], in0=delta[dt][:],
                    scalar1=A_sb[:, dt * N + n:dt * N + n + 1], scalar2=None, op0=ALU.mult))
            S.need(A_, V)
            S.emit(A_, lambda E: E.activation(out=scanA[:, 8 * TC:16 * TC], in_=scanA[:, 8 * TC:16 * TC], func=AF.Exp))
            exp_ms2 = S.milestone(A_)
            S.wait(V, S.sem[A_], exp_ms1)
            S.emit(V, lambda E: E.tensor_tensor(
                out=scanB[:].bitcast(BF16)[:, 0:8 * TC].rearrange("p (n t) -> p n t", n=8),
                in0=scr1[:].bitcast(BF16)[:, 0:TC].unsqueeze(1).broadcast_to([128, 8, TC]),
                in1=Brep[:].bitcast(BF16)[:, 0:8 * TC].rearrange("p (n t) -> p n t", n=8), op=ALU.mult))
            S.wait(V, S.sem[A_], exp_ms2)
            S.emit(V, lambda E: E.tensor_tensor(
                out=scanB[:].bitcast(BF16)[:, 8 * TC:16 * TC].rearrange("p (n t) -> p n t", n=8),
                in0=scr1[:].bitcast(BF16)[:, 0:TC].unsqueeze(1).broadcast_to([128, 8, TC]),
                in1=Brep[:].bitcast(BF16)[:, 8 * TC:16 * TC].rearrange("p (n t) -> p n t", n=8), op=ALU.mult))
            S.vguard()
            if b == 0 and k == dbg_k and dt == 0:
                dbg_dump(2, hcarry[0][:], N)
            S.emit(V, lambda E, dt=dt: E.tensor_tensor(out=injb[:], in0=scanA[:, 0::TC], in1=hcarry[dt][:], op=ALU.mult))
            S.vguard()
            S.emit(V, lambda E: E.tensor_tensor(out=scanB[:].bitcast(BF16)[:, 0:N * TC][:, 0::TC],
                                                in0=scanB[:].bitcast(BF16)[:, 0:N * TC][:, 0::TC], in1=injb[:], op=ALU.add))
            S.vguard()
            S.emit(V, lambda E: E.memset(scanA[:, 0::TC], 0.0))
            S.vguard()
            if b == 0 and k == dbg_k and dt == 0:
                dbg_dump(3, scanA[:], 4096)
                dbg_dump(4, scanB[:], 4096)
            S.emit(V, lambda E: E.tensor_tensor_scan(out=scanH[:], data0=scanA[:],
                                                     data1=scanB[:].bitcast(BF16)[:, 0:N * TC],
                                                     initial=0.0, op0=ALU.mult, op1=ALU.add))
            S.vguard()
            if b == 0 and k == dbg_k and dt == 0:
                dbg_dump(5, scanH[:], 4096)
            S.emit(V, lambda E, dt=dt: E.tensor_copy(out=hcarry[dt][:], in_=scanH[:, TC - 1::TC]))
            S.emit(V, lambda E: E.tensor_tensor(out=hC_bf[:, 0:4096], in0=scanH[:], in1=Crep[:], op=ALU.mult))
            S.emit(V, lambda E: E.tensor_tensor(out=hC_bf[:, 0:8 * TC], in0=hC_bf[:, 0:8 * TC], in1=hC_bf[:, 8 * TC:16 * TC], op=ALU.add))
            S.emit(V, lambda E: E.tensor_tensor(out=hC_bf[:, 0:4 * TC], in0=hC_bf[:, 0:4 * TC], in1=hC_bf[:, 4 * TC:8 * TC], op=ALU.add))
            S.emit(V, lambda E: E.tensor_tensor(out=hC_bf[:, 0:2 * TC], in0=hC_bf[:, 0:2 * TC], in1=hC_bf[:, 2 * TC:4 * TC], op=ALU.add))
            S.emit(V, lambda E: E.tensor_tensor(out=hC_bf[:, 0:TC], in0=hC_bf[:, 0:TC], in1=hC_bf[:, TC:2 * TC], op=ALU.add))
            S.emit(V, lambda E, dt=dt: E.scalar_tensor_tensor(out=scr1[:], in0=ub[dt], scalar=Dsk_sb[:, dt:dt + 1],
                                                              in1=hC_bf[:, 0:TC], op0=ALU.mult, op1=ALU.add))
            S.emit(V, lambda E, dt=dt, slot=slot: E.tensor_tensor(out=xb[slot][dt][:], in0=xb[slot][dt][:], in1=scr1[:], op=ALU.add))
            if b == 0 and k == dbg_k and dt == 0:
                dbg_dump(6, scanA[:, 0:256], 256)
                dbg_dump(7, xb[slot][0][:], 256)
        brep_free_v[0] = S.milestone(V)
        tms2 = ln_sums(xb[slot], PS_LN0, PS_LN1)
        ln_rest(xb[slot], ub, None, PS_LN0, PS_LN1, tms2)
        S.emit(V, lambda E: E.reduce_max(
            out=tmax[:, 0:TC],
            in_=ubig[:].rearrange("p (dt t) -> p dt t", t=TC).transpose([0, 2, 1]),
            axis=AX.X, apply_absolute_value=True))
        quant_cols(2, 0, srow, tmax, TC, dbg_on=(debug and b == 0 and k == dbg_k))
        S.need(T_, V)
        bank_guard(T_, bank_state, PS_LN1)
        S.emit(T_, lambda E: E.matmul(out=ps[PS_LN1][:, 0:256], lhsT=ones_rowf[:], rhs=srow[:, 0:256], start=True, stop=True))
        S.emit(T_, lambda E: E.matmul(out=ps[PS_LN1][:, 256:512], lhsT=ones_rowf[:], rhs=srow[:, 384:640], start=True, stop=True))
        S.need(V, T_)
        S.emit(V, lambda E: E.tensor_copy(out=s1_rep[:], in_=ps[PS_LN1][:, 0:512]))
        bank_state[PS_LN1] = (V, S.milestone(V))
        if b == 0 and k == dbg_k:
            dbg_dump(12, s1_rep[:], 512)
        for dt in range(NDT):
            S.emit(V, lambda E, dt=dt: E.tensor_tensor(out=scr2[:], in0=ub[dt], in1=s1_rep[:, 0:256], op=ALU.mult))
            S.emit(V, lambda E, dt=dt, uslot=uslot: E.tensor_scalar(out=ubf[uslot][dt][:], in0=scr2[:], scalar1=MAGIC, scalar2=MAGIC,
                                                       op0=ALU.add, op1=ALU.subtract))
        vq_ms[g] = S.milestone(V)

    def ffn_p1(b, k, g):
        slot = g % 3
        pslot = (g - 1) % 3
        uslot = k % 2
        if k == 0:
            S.emit(V, lambda E: E.memset(stash[:], 0.0))
            S.vguard()
        if k == 0:
            tw0, L, ilo = 0, TC - 1, 2
        elif k == NCH - 1:
            tw0, L, ilo = k * TC - 1, TC + 1, 1
        else:
            tw0, L, ilo = k * TC - 1, TC, 1
        S.wait(T_, S.sem[V], vq_ms[g])
        S.emit(V, lambda E, L=L: E.memset(tmax[:, 0:384], 0.0))
        cw1 = {}
        t_ms1 = {}
        cw1[0] = S.dma(SY, w1c[0][:], w1ts[0].rearrange("p dk c -> p (dk c)"))
        cw1[1] = S.dma(SY, w1c[1][:], w1ts[1].rearrange("p dk c -> p (dk c)"))
        cw1[2] = S.dma(SY, w1c[2][:], w1ts[2].rearrange("p dk c -> p (dk c)"))
        for ht in range(NHT):
            pp3 = ht % 3
            pp = ht % 2
            S.wait_dma(T_, SY, cw1[ht])
            bank = PS_F1A if pp == 0 else PS_F1B
            bank_guard(T_, bank_state, bank)
            for dk in range(NDT):
                S.emit(T_, lambda E, dk=dk, pp3=pp3, bank=bank, uslot=uslot: E.matmul(
                    out=ps[bank][:, 0:TC], lhsT=w1c[pp3][:, dk * 128:(dk + 1) * 128],
                    rhs=ubf[uslot][dk][:], start=(dk == 0), stop=(dk == NDT - 1)))
            t_ms1[ht] = S.milestone(T_)
            if ht + 3 < NHT:
                S.wait(SY, S.sem[T_], t_ms1[ht])
                cw1[ht + 3] = S.dma(SY, w1c[(ht + 3) % 3][:],
                                    w1ts[ht + 3].rearrange("p dk c -> p (dk c)"))
            S.wait(A_, S.sem[T_], t_ms1[ht])
            S.emit(A_, lambda E, pp=pp, bank=bank: E.copy(out=h1r[pp][:], in_=ps[bank][:, 0:TC]))
            cp_ms = S.milestone(A_)
            bank_state[bank] = (A_, cp_ms)
            S.wait(V, S.sem[A_], cp_ms)
            S.emit(V, lambda E, ht=ht, pp=pp: E.tensor_copy(out=h1buf[pp][:, 0:2], in_=stash[:, ht * 2:ht * 2 + 2]))
            S.emit(V, lambda E, pp=pp: E.tensor_tensor(out=h1buf[pp][:, 2:2 + TC], in0=h1r[pp][:],
                                                       in1=s1_rep[:, 256:512], op=ALU.mult))
            S.vguard()
            S.emit(V, lambda E, ht=ht, pp=pp: E.tensor_copy(out=stash[:, ht * 2:ht * 2 + 2], in_=h1buf[pp][:, TC:TC + 2]))
            if debug and b == 0 and k == dbg_k and ht == 20:
                dbg_dump(25, h1buf[pp][:, 0:259], 259)
            c2o = 300 + (ht % 2) * 300
            S.emit(V, lambda E, ht=ht, pp=pp, L=L, ilo=ilo: E.scalar_tensor_tensor(
                out=scr3[:, 0:L], in0=h1buf[pp][:, ilo - 1:ilo - 1 + L], scalar=convc[:, ht:ht + 1],
                in1=h1buf[pp][:, ilo:ilo + L], op0=ALU.mult, op1=ALU.add))
            S.emit(V, lambda E, ht=ht, pp=pp, L=L, ilo=ilo, c2o=c2o: E.scalar_tensor_tensor(
                out=scr3[:, c2o:c2o + L], in0=h1buf[pp][:, ilo + 1:ilo + 1 + L], scalar=convc[:, NHT + ht:NHT + ht + 1],
                in1=scr3[:, 0:L], op0=ALU.mult, op1=ALU.add))
            if b == 0 and k == dbg_k and ht == 0:
                dbg_dump(15, h1buf[0][:, 0:259], 259)
                dbg_dump(16, scr3[:, c2o:c2o + L], L)
            S.need(A_, V)
            S.emit(A_, lambda E, ht=ht, L=L, c2o=c2o: E.activation(
                out=hs[ht][:, 0:L], in_=scr3[:, c2o:c2o + L], func=AF.Silu,
                scale=convc[:, 2 * NHT + ht:2 * NHT + ht + 1], bias=convc[:, 3 * NHT + ht:3 * NHT + ht + 1]))
            if b == 0 and k == dbg_k and ht == 0:
                dbg_dump(17, hs[0][:, 0:257], 257)
        return L

    def ffn_p2(b, k, g, L):
        slot = g % 3
        pslot = (g - 1) % 3
        uslot = k % 2
        S.need(V, A_)
        S.emit(V, lambda E, L=L: E.reduce_max(
            out=tmax[:, 0:L],
            in_=scanA[:, 0:15 * 258].rearrange("p (ht t) -> p ht t", t=258)[:, :, 0:L].transpose([0, 2, 1]),
            axis=AX.X, apply_absolute_value=True))
        S.emit(V, lambda E, L=L: E.reduce_max(
            out=scr3[:, 0:L],
            in_=scanB[:, 0:15 * 258].rearrange("p (ht t) -> p ht t", t=258)[:, :, 0:L].transpose([0, 2, 1]),
            axis=AX.X, apply_absolute_value=True))
        S.emit(V, lambda E, L=L: E.tensor_tensor(out=tmax[:, 0:L], in0=tmax[:, 0:L], in1=scr3[:, 0:L], op=ALU.max))
        S.emit(V, lambda E, L=L: E.reduce_max(
            out=scr3[:, 0:L],
            in_=hs_x[:, 0:516].rearrange("p (ht t) -> p ht t", t=258)[:, :, 0:L].transpose([0, 2, 1]),
            axis=AX.X, apply_absolute_value=True))
        S.emit(V, lambda E, L=L: E.tensor_tensor(out=tmax[:, 0:L], in0=tmax[:, 0:L], in1=scr3[:, 0:L], op=ALU.max))
        if b == 0 and k == dbg_k:
            dbg_dump(23, hs[20][:, 0:257], 257)
            dbg_dump(18, tmax[:, 0:384], 384)
        quant_cols(3, 1, srow, tmax, L)
        S.need(T_, V)
        bank_guard(T_, bank_state, PS_LN0)
        bank_guard(T_, bank_state, PS_LN1)
        S.emit(T_, lambda E: E.matmul(out=ps[PS_LN1][:, 0:384], lhsT=ones_rowf[:], rhs=srow[:, 0:384], start=True, stop=True))
        S.emit(T_, lambda E: E.matmul(out=ps[PS_LN0][:, 0:384], lhsT=ones_rowf[:], rhs=srow[:, 384:768], start=True, stop=True))
        S.need(V, T_)
        S.emit(V, lambda E: E.tensor_copy(out=s2_rep[:], in_=ps[PS_LN1][:, 0:384]))
        S.emit(V, lambda E: E.tensor_copy(out=si2_rep[:], in_=ps[PS_LN0][:, 0:384]))
        if b == 0 and k == dbg_k:
            dbg_dump(19, s2_rep[:], 384)
            dbg_dump(20, si2_rep[:], 384)
        bank_state[PS_LN0] = (V, S.milestone(V))
        bank_state[PS_LN1] = (V, S.cnt[V])
        for ht in range(NHT):
            S.emit(V, lambda E, ht=ht, L=L: E.tensor_tensor(out=scr3[:, 0:L], in0=hs[ht][:, 0:L], in1=s2_rep[:, 0:L], op=ALU.mult))
            S.emit(V, lambda E, ht=ht, L=L: E.tensor_scalar(out=hq[ht][:, 0:L], in0=scr3[:, 0:L],
                                                            scalar1=MAGIC, scalar2=MAGIC, op0=ALU.add, op1=ALU.subtract))
        cw2 = {}
        f2_ms = {}
        cw2[0] = S.dma(SY, w2c[0][:], w2ts[0].rearrange("p ht c -> p (ht c)"))
        mv_q = S.milestone(V)
        for dt in range(NDT):
            pp = dt % 2
            if dt + 1 < NDT:
                if dt >= 1:
                    S.wait(SY, S.sem[T_], S.cnt[T_])
                cw2[dt + 1] = S.dma(SY, w2c[(dt + 1) % 2][:],
                                    w2ts[dt + 1].rearrange("p ht c -> p (ht c)"))
            if debug and b == 0 and k == dbg_k and dt == 0:
                S.wait(G, S.sem[V], mv_q)
                S.dma(G, dbg_h[21, :, 0:257], hq[0])
                S.wait_dma(G, SY, cw2[0])
                S.dma(G, dbg_h[22, :, 0:4096], w2c[0][:])
            S.wait_dma(T_, SY, cw2[dt])
            if dt == 0:
                S.wait(T_, S.sem[V], mv_q)
            bank = PS_F2A if pp == 0 else PS_F2B
            bank_guard(T_, bank_state, bank)
            for ht in range(NHT):
                S.emit(T_, lambda E, ht=ht, pp=pp, bank=bank, L=L: E.matmul(
                    out=ps[bank][:, 0:L], lhsT=w2c[pp][:, ht * 128:(ht + 1) * 128],
                    rhs=hq[ht][:, 0:L], start=(ht == 0), stop=(ht == NHT - 1)))
            S.need(A_, T_)
            S.emit(A_, lambda E, dt=dt, bank=bank, L=L: E.copy(out=f2s[dt][:, 0:L], in_=ps[bank][:, 0:L]))
            f2_ms[dt] = S.milestone(A_)
            bank_state[bank] = (A_, f2_ms[dt])
        return f2_ms, L

    def ffn_tail(b, k, g, ctx2):
        slot = g % 3
        pslot = (g - 1) % 3
        f2_ms, L = ctx2
        for dt in range(NDT):
            S.wait(V, S.sem[A_], f2_ms[dt])
            S.emit(V, lambda E, dt=dt, L=L: E.tensor_tensor(out=scr3[:, 0:L], in0=f2s[dt][:, 0:L],
                                                            in1=si2_rep[:, 0:L], op=ALU.mult))
            if k == 0:
                S.emit(V, lambda E, dt=dt, slot=slot, L=L: E.tensor_tensor(out=xb[slot][dt][:, 0:L], in0=xb[slot][dt][:, 0:L],
                                                                in1=scr3[:, 0:L], op=ALU.add))
            else:
                S.emit(V, lambda E, dt=dt, pslot=pslot: E.tensor_tensor(
                    out=xb[pslot][dt][:, TC - 1:TC], in0=xb[pslot][dt][:, TC - 1:TC], in1=scr3[:, 0:1], op=ALU.add))
                S.emit(V, lambda E, dt=dt, slot=slot, L=L: E.tensor_tensor(
                    out=xb[slot][dt][:, 0:L - 1], in0=xb[slot][dt][:, 0:L - 1], in1=scr3[:, 1:L], op=ALU.add))
        mv = S.milestone(V)
        S.wait(G, S.sem[V], mv)
        cg = 0
        if k >= 1:
            for dt in range(NDT):
                cg = S.dma(G, xo_hbm[b, dt * 128:(dt + 1) * 128, (k - 1) * TC:k * TC], xb[pslot][dt][:])
            xout_counts[g - 1] = cg
        if k == NCH - 1:
            for dt in range(NDT):
                cg = S.dma(G, xo_hbm[b, dt * 128:(dt + 1) * 128, k * TC:(k + 1) * TC], xb[slot][dt][:])
            xout_counts[g] = cg

    chunks = [(b, k) for b in range(nb) for k in range(NCH)]
    pend_tail = None
    for g, (b, k) in enumerate(chunks):
        S.squeue_alias = "s" if g % 2 == 0 else "s1"
        if pend_tail is not None:
            ffn_tail(*pend_tail)
            pend_tail = None
        tms = front_a(b, k, g)
        ctx = mixer_front(b, k, g, tms)
        if g > 0:
            pb, pk = chunks[g - 1]
            Lp = ffn_p1(pb, pk, g - 1)
            ctx2 = ffn_p2(pb, pk, g - 1, Lp)
            pend_tail = (pb, pk, g - 1, ctx2)
        mixer_back(b, k, g, ctx)
    if pend_tail is not None:
        ffn_tail(*pend_tail)
    gl = len(chunks) - 1
    Lp = ffn_p1(chunks[-1][0], chunks[-1][1], gl)
    ctx2 = ffn_p2(chunks[-1][0], chunks[-1][1], gl, Lp)
    ffn_tail(chunks[-1][0], chunks[-1][1], gl, ctx2)
    S.wait(G, S.dsem["g"], S.dcnt["g"])
    S.wait(G, S.dsem["s"], S.dcnt["s"])
    S.wait(G, S.dsem["s1"], S.dcnt["s1"])
    S.wait(G, S.dsem["a"], S.dcnt["a"])

    with nc.Block() as block:
        S.run(block)
    stack.close()
    return nc


_NC_CACHE = {}


def kernel(L0_attn_x, L0_attn_A_log, L0_attn_D_skip, L0_attn_Wd, L0_attn_bd,
           L0_attn_Wb, L0_attn_bb, L0_attn_Wc, L0_attn_bc,
           L0_ffn_w1, L0_ffn_conv_w, L0_ffn_conv_b, L0_ffn_w2):
    x = np.asarray(L0_attn_x, dtype=np.float32)
    NB_CORES = 8
    T = x.shape[1]
    if "nc" not in _NC_CACHE:
        _NC_CACHE["nc"] = build_program(T=T)
    nc = _NC_CACHE["nc"]
    common = {
        "A_log": np.ascontiguousarray(L0_attn_A_log, dtype=np.float32),
        "Dskip": np.ascontiguousarray(L0_attn_D_skip, dtype=np.float32),
        "Wd": np.ascontiguousarray(L0_attn_Wd, dtype=np.float32),
        "bd": np.ascontiguousarray(np.asarray(L0_attn_bd, dtype=np.float32).reshape(1, D)),
        "Wb": np.ascontiguousarray(L0_attn_Wb, dtype=np.float32),
        "bb": np.ascontiguousarray(np.asarray(L0_attn_bb, dtype=np.float32).reshape(1, N)),
        "Wc": np.ascontiguousarray(L0_attn_Wc, dtype=np.float32),
        "bc": np.ascontiguousarray(np.asarray(L0_attn_bc, dtype=np.float32).reshape(1, N)),
        "w1": np.ascontiguousarray(L0_ffn_w1, dtype=np.float32),
        "cw": np.ascontiguousarray(np.asarray(L0_ffn_conv_w, dtype=np.float32).reshape(H, 3)),
        "cb": np.ascontiguousarray(np.asarray(L0_ffn_conv_b, dtype=np.float32).reshape(H)),
        "w2": np.ascontiguousarray(L0_ffn_w2, dtype=np.float32),
        "ident": np.eye(128, dtype=np.float32),
    }
    in_maps = []
    for c in range(NB_CORES):
        xc = np.ascontiguousarray(x[B_LOC * c:B_LOC * (c + 1)].transpose(0, 2, 1))
        m = dict(common)
        m["x"] = xc
        in_maps.append(m)
    res = run_bass_kernel_spmd(nc, in_maps, core_ids=list(range(NB_CORES)))
    outs = [np.asarray(res.results[c]["xo"]).transpose(0, 2, 1) for c in range(NB_CORES)]
    return np.ascontiguousarray(np.concatenate(outs, axis=0)).astype(np.float32)
